# version 05
# baseline (speedup 1.0000x reference)
import os
import numpy as np
import ml_dtypes
import concourse.bass as bass
import concourse.mybir as mybir
from concourse.bass_core_rust import IndirectOffsetOnAxis
from concourse.bass_utils import run_bass_kernel_spmd
from contextlib import ExitStack

F32 = mybir.dt.float32
BF16 = mybir.dt.bfloat16
U32 = mybir.dt.uint32
AF = mybir.ActivationFunctionType
ALU = mybir.AluOpType
AX = mybir.AxisListType.X
BF = ml_dtypes.bfloat16

S = 4096
E = 1024
NP = 8
NEXP = 32
HEXP = 512
CAP = 1024
NSH = 36736
DFLAT = NSH + 128
NT = 32


class Ch:
    def __init__(self, nc, st, name):
        self.h = st.enter_context(nc.semaphore(name))
        self.n = 0
        self.marks = {}

    def inc(self, instr, k=1):
        instr.then_inc(self.h, k)
        self.n += k
        return instr

    def mark(self, key):
        self.marks[key] = self.n

    def wait(self, eng, key=None):
        v = self.n if key is None else self.marks[key]
        if v > 0:
            eng.wait_ge(self.h, v)

    def wait_val(self, eng, v):
        if v > 0:
            eng.wait_ge(self.h, v)


def dap(t, off, dims):
    return bass.AP(t.ap().tensor, off, dims)


def build_nc(phases=6):
    ksub = int(os.environ.get("KSUB", "99"))
    ks5 = int(os.environ.get("KS5", "99"))
    npr = int(os.environ.get("NPR", "8"))
    nqt = int(os.environ.get("NQT", "4"))
    nc = bass.Bass("TRN2", target_bir_lowering=False)
    import functools
    _orig_mm = type(nc.tensor).matmul

    def _mm(self, *a, **kw):
        kw.setdefault("skip_group_check", True)
        return _orig_mm(self, *a, **kw)

    nc.tensor.matmul = functools.partial(_mm, nc.tensor)

    d = {}
    d["x"] = nc.dram_tensor("x", [S, E], F32, kind="ExternalInput")
    for nm, shp in [("wq", [E, E]), ("wk", [E, E]), ("wv", [E, E]),
                    ("rel", [2047, 16]), ("gamma_r", [8, 128]), ("beta_r", [8, 128]),
                    ("wg", [E, NEXP]),
                    ("w1", [NEXP, E, HEXP]), ("w2", [NEXP, HEXP, E]),
                    ("w3", [NEXP, E, HEXP]),
                    ("ws1", [E, E]), ("ws2", [E, E]), ("ws3", [E, E])]:
        d[nm] = nc.dram_tensor(nm, shp, F32, kind="ExternalInput")
    d["ident_bf"] = nc.dram_tensor("ident_bf", [128, 128], BF16, kind="ExternalInput")
    d["ident_f32"] = nc.dram_tensor("ident_f32", [128, 128], F32, kind="ExternalInput")
    d["lstr_f32"] = nc.dram_tensor("lstr_f32", [128, 128], F32, kind="ExternalInput")
    d["ones_bf"] = nc.dram_tensor("ones_bf", [128, 1], BF16, kind="ExternalInput")
    d["ones_f32"] = nc.dram_tensor("ones_f32", [128, 1], F32, kind="ExternalInput")
    d["onesrow_f32"] = nc.dram_tensor("onesrow_f32", [1, 128], F32, kind="ExternalInput")
    d["m16_bf"] = nc.dram_tensor("m16_bf", [128, 16], BF16, kind="ExternalInput")
    d["e1024"] = nc.dram_tensor("e1024", [1, 32], F32, kind="ExternalInput")
    d["xn_hbm"] = nc.dram_tensor("xn_hbm", [S, E], BF16, kind="Internal")
    d["att_hbm"] = nc.dram_tensor("att_hbm", [NP, 128, S], BF16, kind="Internal")
    d["s_hbm"] = nc.dram_tensor("s_hbm", [S, E], BF16, kind="Internal")
    d["dflat"] = nc.dram_tensor("dflat", [DFLAT], BF16, kind="Internal")
    d["xe_hbm"] = nc.dram_tensor("xe_hbm", [NEXP * CAP, E], BF16, kind="Internal")
    d["oute_hbm"] = nc.dram_tensor("oute_hbm", [NEXP * CAP, E], BF16, kind="Internal")
    d["out"] = nc.dram_tensor("out", [S, E], F32, kind="ExternalOutput")

    sg = ExitStack()
    ss = ExitStack()

    xnT = [sg.enter_context(nc.sbuf_tensor(f"xnT{i}", [128, S], BF16)) for i in range(8)]
    ident_bf = sg.enter_context(nc.sbuf_tensor("identb", [128, 128], BF16))
    ident_f = sg.enter_context(nc.sbuf_tensor("identf", [128, 128], F32))
    lstr = sg.enter_context(nc.sbuf_tensor("lstr", [128, 128], F32))
    ones_bf = sg.enter_context(nc.sbuf_tensor("onesb", [128, 1], BF16))
    ones_f = sg.enter_context(nc.sbuf_tensor("onesf", [128, 1], F32))
    onesrow = sg.enter_context(nc.sbuf_tensor("onesrow", [1, 128], F32))
    m16 = sg.enter_context(nc.sbuf_tensor("m16", [128, 16], BF16))
    e1024 = sg.enter_context(nc.sbuf_tensor("e1024s", [1, 32], F32))
    rtcol16 = sg.enter_context(nc.sbuf_tensor("rtcol16", [16, 1], F32))
    gammaT = sg.enter_context(nc.sbuf_tensor("gammaT", [128, 8], F32))
    betaT = sg.enter_context(nc.sbuf_tensor("betaT", [128, 8], F32))
    slotbuf = sg.enter_context(nc.sbuf_tensor("slotbuf", [128, 128], U32))
    w4buf = sg.enter_context(nc.sbuf_tensor("w4buf", [128, 128], F32))

    cLD = Ch(nc, ss, "cLD")
    cGL = Ch(nc, ss, "cGL")
    cPE = Ch(nc, ss, "cPE")
    cACT = Ch(nc, ss, "cACT")
    cDVE = Ch(nc, ss, "cDVE")
    cST = Ch(nc, ss, "cST")
    cXE = Ch(nc, ss, "cXE")
    cEXP = Ch(nc, ss, "cEXP")
    cSC = Ch(nc, ss, "cSC")
    cAV = Ch(nc, ss, "cAV")
    cW = Ch(nc, ss, "cW")
    cOUT = Ch(nc, ss, "cOUT")
    cFD = Ch(nc, ss, "cFD")
    cFA = Ch(nc, ss, "cFA")

    def fd(e, instr):
        cFD.inc(instr)
        e.wait_ge(cFD.h, cFD.n)

    def fa(e, instr):
        cFA.inc(instr)
        e.wait_ge(cFA.h, cFA.n)


    def _drain(nc_, chans_by_eng):
        with nc_.Block() as bdr:
            for engname, chans in chans_by_eng.items():
                def f(e, chans=chans):
                    for ch in chans:
                        if ch.n > 0:
                            e.wait_ge(ch.h, ch.n)
                getattr(bdr, engname)(f)

    def _full_drain():
        _drain(nc, {"sync": [cLD, cST, cOUT], "gpsimd": [cGL, cW],
                    "scalar": [cXE]})

    st0 = ExitStack()
    xt = [st0.enter_context(nc.sbuf_tensor(f"xt{i}", [128, E], F32)) for i in range(2)]
    xnt = [st0.enter_context(nc.sbuf_tensor(f"xnt{i}", [128, E], BF16)) for i in range(2)]
    scrap = st0.enter_context(nc.sbuf_tensor("scrap", [128, E], F32))
    stat = st0.enter_context(nc.sbuf_tensor("stat", [128, 8], F32))
    rta = st0.enter_context(nc.sbuf_tensor("rta", [128, 256], F32))
    rtd = st0.enter_context(nc.sbuf_tensor("rtd", [128, 256], BF16))
    rtrow = st0.enter_context(nc.sbuf_tensor("rtrow", [1, 16], F32))
    rtb = st0.enter_context(nc.sbuf_tensor("rtb", [128, 16], F32))
    rtb256 = st0.enter_context(nc.sbuf_tensor("rtb256", [128, 256], F32))
    gbl = st0.enter_context(nc.sbuf_tensor("gbl", [128, 256], F32))
    zt = st0.enter_context(nc.sbuf_tensor("zt", [128, 4096], BF16))
    ps0 = st0.enter_context(nc.psum_tensor("ps0", [128, 512], F32))
    psT0 = st0.enter_context(nc.psum_tensor("psT0", [128, 1024], BF16))

    with nc.Block() as b:
        def z_init(e):
            cDVE.inc(e.memset(rta[:], 0.0))
            cDVE.mark("rta0")
            cDVE.inc(e.memset(zt[:], 0.0))
            cDVE.mark("zt")
        b.vector(z_init)

        def ld_consts(e):
            for t, src in [(ident_bf, d["ident_bf"]), (ident_f, d["ident_f32"]),
                           (lstr, d["lstr_f32"]), (ones_bf, d["ones_bf"]),
                           (ones_f, d["ones_f32"]), (onesrow, d["onesrow_f32"]),
                           (m16, d["m16_bf"]), (e1024, d["e1024"])]:
                cLD.inc(e.dma_start(out=t[:], in_=src.ap()[:]), 16)
            cLD.inc(e.dma_start(out=rtcol16[:],
                                in_=dap(d["rel"], 2046 * 16, [[1, 16], [1, 1]])), 16)
            cLD.inc(e.dma_start(out=rtrow[:],
                                in_=dap(d["rel"], 2046 * 16, [[16, 1], [1, 16]])), 16)
            cDVE.wait(e, "rta0")
            cLD.inc(e.dma_start(out=rta[0:127, :],
                                in_=dap(d["rel"], 0, [[256, 127], [1, 256]])), 16)
            cLD.inc(e.dma_start(out=rta[127:128, 0:240],
                                in_=dap(d["rel"], 127 * 256, [[240, 1], [1, 240]])), 16)
            cLD.inc(e.dma_start(out=gbl[0:8, 0:128], in_=d["gamma_r"].ap()[:]), 16)
            cLD.inc(e.dma_start(out=gbl[0:8, 128:256], in_=d["beta_r"].ap()[:]), 16)
            cLD.mark("consts")
        b.sync(ld_consts)

        if ksub < 2:
            _noop = True
        def pre_pe(e):
            cLD.wait(e, "consts")
            e.transpose(ps0[:, 0:8], gbl[0:8, 0:128], ident_f[0:8, 0:8])
            cPE.inc(e.transpose(ps0[:, 8:16], gbl[0:8, 128:256], ident_f[0:8, 0:8]))
            cPE.mark("gbT")
            cPE.inc(e.matmul(ps0[:, 256:272], onesrow[0:1, :], rtrow[:],
                             start=True, stop=True))
            cPE.mark("rtb")
        if ksub >= 2: b.tensor(pre_pe)

        def pre_dve(e):
            cPE.wait(e, "rtb")
            e.tensor_copy(gammaT[:], ps0[:, 0:8])
            e.tensor_copy(betaT[:], ps0[:, 8:16])
            fd(e, e.tensor_copy(rtb[:], ps0[:, 256:272]))
            for u in range(16):
                ins = e.tensor_copy(rtb256[:, 16 * u:16 * u + 16], rtb[:])
            fd(e, ins)
            cDVE.inc(e.tensor_sub(rtd[:], rta[:], rtb256[:]))
            cDVE.mark("rtd")
        if ksub >= 21: b.vector(pre_dve)

        def pre_st(e):
            cDVE.wait(e, "rtd")
            cST.inc(e.dma_start(out=dap(d["dflat"], 0, [[256, 127], [1, 256]]),
                                in_=rtd[0:127, :]), 16)
            cST.inc(e.dma_start(out=dap(d["dflat"], 127 * 256, [[240, 1], [1, 240]]),
                                in_=rtd[127:128, 0:240]), 16)
            cDVE.wait(e, "zt")
            nz = (DFLAT - 32752 + 1) // 2
            cST.inc(e.dma_start(out=dap(d["dflat"], 32752, [[nz, 2], [1, nz]]),
                                in_=zt[0:2, 0:nz]), 16)
            cST.mark("dflat")
        if ksub >= 22: b.sync(pre_st)

        def zero_fill(e):
            cDVE.wait(e, "zt")
            for ch in range(64):
                cGL.inc(e.dma_start(out=dap(d["xe_hbm"], ch * 128 * 4096,
                                            [[4096, 128], [1, 4096]]),
                                    in_=zt[:]), 16)
            cGL.mark("xezero")
        if ksub >= 23: b.gpsimd(zero_fill)

        for tt in range(NT):
            def ln_sync(e, tt=tt):
                if tt >= 2:
                    cDVE.wait(e, ("xnv", tt - 2))
                cLD.inc(e.dma_start(out=xt[tt % 2][:],
                                    in_=d["x"].ap()[128 * tt:128 * (tt + 1), :]), 16)
                cLD.mark(("x", tt))
            if ksub >= 24: b.sync(ln_sync)

            def ln_v1(e, tt=tt):
                cLD.wait(e, ("x", tt))
                e.scalar_tensor_tensor(scrap[:], xt[tt % 2][:], 1.0, xt[tt % 2][:],
                                       ALU.mult, ALU.mult, accum_out=stat[:, 0:1])
                e.reduce_sum(stat[:, 1:2], xt[tt % 2][:], axis=AX)
                e.tensor_scalar(stat[:, 2:3], stat[:, 0:1], 1.0 / E, 1e-5,
                                ALU.mult, op1=ALU.add)
                e.tensor_tensor(stat[:, 5:6], stat[:, 1:2], stat[:, 1:2], ALU.mult)
                e.tensor_scalar(stat[:, 6:7], stat[:, 1:2], -1.0 / E, None, ALU.mult)
                cDVE.inc(e.scalar_tensor_tensor(stat[:, 2:3], stat[:, 5:6],
                                                -1.0 / (E * E), stat[:, 2:3],
                                                ALU.mult, ALU.add))
                cDVE.mark(("var", tt))
            if ksub >= 24: b.vector(ln_v1)

            def ln_a(e, tt=tt):
                cDVE.wait(e, ("var", tt))
                fa(e, e.activation(out=stat[:, 4:5], in_=stat[:, 2:3], func=AF.Ln))
                cACT.inc(e.activation(out=stat[:, 3:4], in_=stat[:, 4:5], func=AF.Exp,
                                      scale=-0.5))
                cACT.mark(("rstd", tt))
            if ksub >= 24: b.scalar(ln_a)

            def ln_v2(e, tt=tt):
                cACT.wait(e, ("rstd", tt))
                if tt >= 2:
                    cST.wait(e, ("xnst", tt - 2))
                    cPE.wait(e, ("xntr", tt - 2))
                cDVE.inc(e.tensor_scalar(xnt[tt % 2][:], xt[tt % 2][:], stat[:, 6:7],
                                         stat[:, 3:4], ALU.add, op1=ALU.mult))
                cDVE.mark(("xnv", tt))
            if ksub >= 24: b.vector(ln_v2)

            def ln_pe(e, tt=tt):
                cDVE.wait(e, ("xnv", tt))
                if tt >= 1:
                    cDVE.wait(e, ("xnev", tt - 1))
                for c in range(8):
                    ins = e.transpose(psT0[:, 128 * c:128 * c + 128],
                                      xnt[tt % 2][:, 128 * c:128 * c + 128], ident_bf[:])
                cPE.inc(ins)
                cPE.mark(("xntr", tt))
            if ksub >= 24: b.tensor(ln_pe)

            def ln_v3(e, tt=tt):
                cPE.wait(e, ("xntr", tt))
                for c in range(8):
                    ins = e.tensor_copy(xnT[c][:, 128 * tt:128 * tt + 128],
                                        psT0[:, 128 * c:128 * c + 128])
                cDVE.inc(ins)
                cDVE.mark(("xnev", tt))
            if ksub >= 24: b.vector(ln_v3)

            def ln_st(e, tt=tt):
                cDVE.wait(e, ("xnv", tt))
                cST.inc(e.dma_start(out=d["xn_hbm"].ap()[128 * tt:128 * (tt + 1), :],
                                    in_=xnt[tt % 2][:]), 16)
                cST.mark(("xnst", tt))
            if ksub >= 24: b.sync(ln_st)
    st0.close()

    if phases < 2:
        _full_drain()
        ss.close(); sg.close(); return nc
    st1 = ExitStack()
    wg_sb = st1.enter_context(nc.sbuf_tensor("wg_sb", [128, 256], BF16))
    probs = st1.enter_context(nc.sbuf_tensor("probs", [128, 32], F32))
    tops = st1.enter_context(nc.sbuf_tensor("tops", [128, 8], F32))
    oh = st1.enter_context(nc.sbuf_tensor("oh", [128, 32], F32))
    selb = st1.enter_context(nc.sbuf_tensor("selb", [128, 32], F32))
    rsm = st1.enter_context(nc.sbuf_tensor("rsm", [128, 4], F32))
    badd = st1.enter_context(nc.sbuf_tensor("badd", [1, 32], F32))
    xnr = [st1.enter_context(nc.sbuf_tensor(f"xnr{i}", [128, E], BF16)) for i in range(2)]
    psL = st1.enter_context(nc.psum_tensor("psL", [128, 512], F32))
    psRr = st1.enter_context(nc.psum_tensor("psRr", [128, 512], F32))
    psCr = st1.enter_context(nc.psum_tensor("psCr", [128, 512], F32))

    with nc.Block() as b:
        def r_init_g(e):
            cGL.inc(e.dma_start(
                out=wg_sb[:],
                in_=dap(d["wg"], 0, [[32, 128], [128 * 32, 8], [1, 32]])), 16)
            cGL.mark("wg")
        b.gpsimd(r_init_g)

        def r_init_v(e):
            cDVE.inc(e.tensor_copy(badd[:], e1024[:]))
            cDVE.mark("badd0")
        b.vector(r_init_v)

        for tt in range(NT):
            def r_sync(e, tt=tt):
                if tt >= 2:
                    cGL.wait(e, ("scat", tt - 2))
                cLD.inc(e.dma_start(out=xnr[tt % 2][:],
                                    in_=d["xn_hbm"].ap()[128 * tt:128 * (tt + 1), :]),
                        16)
                cLD.mark(("xnr", tt))
            b.sync(r_sync)

            def r_pe1(e, tt=tt):
                cGL.wait(e, "wg")
                if tt >= 1:
                    cACT.wait(e, ("pexp", tt - 1))
                for et in range(8):
                    ins = e.matmul(psL[:, 0:32], xnT[et][:, 128 * tt:128 * tt + 128],
                                   wg_sb[:, 32 * et:32 * et + 32],
                                   start=(et == 0), stop=(et == 7))
                cPE.inc(ins)
                cPE.mark(("lg", tt))
            b.tensor(r_pe1)

            def r_v1(e, tt=tt):
                cPE.wait(e, ("lg", tt))
                cDVE.inc(e.reduce_max(rsm[:, 0:1], psL[:, 0:32], axis=AX, negate=True))
                cDVE.mark(("negm", tt))
            b.vector(r_v1)

            def r_a1(e, tt=tt):
                cDVE.wait(e, ("negm", tt))
                cACT.inc(e.activation(out=probs[:], in_=psL[:, 0:32], func=AF.Exp,
                                      bias=rsm[:, 0:1]))
                cACT.mark(("pexp", tt))
            b.scalar(r_a1)

            def r_v2(e, tt=tt):
                cACT.wait(e, ("pexp", tt))
                fd(e, e.reduce_sum(rsm[:, 1:2], probs[:], axis=AX))
                fd(e, e.reciprocal(rsm[:, 2:3], rsm[:, 1:2]))
                fd(e, e.tensor_scalar(probs[:], probs[:], rsm[:, 2:3], None,
                                      ALU.mult))
                fd(e, e.max(out=tops[:], in_=probs[:]))
                cDVE.inc(e.tensor_copy(w4buf[:, 4 * tt:4 * tt + 4], tops[:, 0:4]))
                cDVE.mark(("top", tt))
            b.vector(r_v2)

            for r in range(4):
                def r_vr(e, tt=tt, r=r):
                    cDVE.wait(e, ("top", tt))
                    cDVE.inc(e.tensor_scalar(oh[:], probs[:], tops[:, r:r + 1], None,
                                             ALU.is_equal))
                    cDVE.mark(("oh", tt, r))
                b.vector(r_vr)

                def r_pr(e, tt=tt, r=r):
                    cDVE.wait(e, ("oh", tt, r))
                    if (tt, r) != (0, 0):
                        cDVE.wait(e, ("slot", tt, r - 1) if r > 0
                                  else ("slot", tt - 1, 3))
                    e.matmul(psRr[:, 0:32], lstr[:], oh[:], start=True, stop=False)
                    e.matmul(psRr[:, 0:32], onesrow[0:1, :], badd[:],
                             start=False, stop=True)
                    cPE.inc(e.matmul(psCr[0:1, 0:32], ones_f[:], oh[:],
                                     start=True, stop=True))
                    cPE.mark(("rs", tt, r))
                b.tensor(r_pr)

                def r_vs(e, tt=tt, r=r):
                    cPE.wait(e, ("rs", tt, r))
                    fd(e, e.tensor_tensor(selb[:], oh[:], psRr[:, 0:32], ALU.mult))
                    fd(e, e.reduce_sum(rsm[:, 3:4], selb[:], axis=AX))
                    fd(e, e.tensor_scalar_min(rsm[:, 3:4], rsm[:, 3:4], 32767.0))
                    fd(e, e.tensor_copy(slotbuf[:, 4 * tt + r:4 * tt + r + 1],
                                        rsm[:, 3:4]))
                    cDVE.inc(e.tensor_tensor(badd[:], badd[:], psCr[0:1, 0:32],
                                             ALU.add))
                    cDVE.mark(("slot", tt, r))
                b.vector(r_vs)

                def r_g(e, tt=tt, r=r):
                    cDVE.wait(e, ("slot", tt, r))
                    cLD.wait(e, ("xnr", tt))
                    cGL.wait(e, "xezero")
                    cGL.inc(e.indirect_dma_start(
                        d["xe_hbm"].ap()[:],
                        IndirectOffsetOnAxis(slotbuf[:, 4 * tt + r:4 * tt + r + 1], 0),
                        xnr[tt % 2][:], None), 16)
                    if r == 3:
                        cGL.mark(("scat", tt))
                b.gpsimd(r_g)
    st1.close()

    if phases < 3:
        _full_drain()
        ss.close(); sg.close(); return nc
    st2 = ExitStack()
    sh = st2.enter_context(nc.sbuf_tensor("sh", [128, NSH], BF16))
    KT = st2.enter_context(nc.sbuf_tensor("KT", [128, S], BF16))
    QT = st2.enter_context(nc.sbuf_tensor("QT", [128, S], BF16))
    V2 = st2.enter_context(nc.sbuf_tensor("V2", [128, 32 * 130], BF16))
    Vt = st2.enter_context(nc.sbuf_tensor("Vt", [128, S], BF16))
    Gbuf = st2.enter_context(nc.sbuf_tensor("Gbuf", [128, 2048], F32))
    wp = [st2.enter_context(nc.sbuf_tensor("wp0", [128, 3 * E], BF16))]
    exf = [st2.enter_context(nc.sbuf_tensor(f"exf{i}", [128, 2048], BF16))
           for i in range(2)]
    ex = [exf[0][:, 0:1024], exf[0][:, 1024:2048],
          exf[1][:, 0:1024], exf[1][:, 1024:2048]]
    att32 = st2.enter_context(nc.sbuf_tensor("att32", [128, 1024], F32))
    rcp = st2.enter_context(nc.sbuf_tensor("rcp", [1, 2048], F32))
    svb = st2.enter_context(nc.sbuf_tensor("svb", [16, 128], F32))
    urow = st2.enter_context(nc.sbuf_tensor("urow", [1, 128], F32))
    usb = st2.enter_context(nc.sbuf_tensor("usb", [128, 1], F32))
    psS = st2.enter_context(nc.psum_tensor("psS", [128, 2048], F32))
    psA = psS

    psAV = st2.enter_context(nc.psum_tensor("psAV", [128, 1024], F32))
    psAVB = st2.enter_context(nc.psum_tensor("psAVB", [128, 1024], F32))

    def gather_plan(pr, qt):
        out = []
        for hi in range(2):
            hh = 2 * pr + hi
            for m in range(16):
                cc = 4095 - 256 * hh + 256 * m - 64 * qt
                a_lo = max(0, cc - 2045)
                a_hi = min(63, cc)
                if a_lo > a_hi:
                    continue
                out.append((64 * hi, m, a_lo, a_hi - a_lo + 1, cc - a_lo))
        return out

    with nc.Block() as b:
        def a_init_v(e):
            cDVE.inc(e.memset(bass.AP(V2.ap().tensor, 64,
                                      [[4160, 128], [65, 64], [1, 1]]), 1.0))
            cDVE.mark("v2ones")
        b.vector(a_init_v)

        def a_init(e):
            cST.wait(e, "dflat")
            cLD.inc(e.dma_start(out=sh[:],
                                in_=dap(d["dflat"], 0, [[1, 128], [1, NSH]])), 16)
            cLD.mark("sh")
        b.sync(a_init)


        for pr in range(npr):
            bufp = 0

            def a_w(e, pr=pr):
                if pr >= 1:
                    cPE.wait(e, ("v", pr - 1, NT - 1))
                for wnm, col in (("wq", 0), ("wk", 1), ("wv", 2)):
                    cW.inc(e.dma_start(
                        out=wp[0][:, E * col:E * col + E],
                        in_=dap(d[wnm], 128 * pr,
                                [[E, 128], [128 * E, 8], [1, 128]])), 16)
                cW.mark(("wp", pr))
            b.gpsimd(a_w)

            base_dve = cDVE.n

            def a_qkv_pe(e, pr=pr, bufp=bufp, base_dve=base_dve):
                cW.wait(e, ("wp", pr))
                cLD.wait(e, "sh")
                if pr >= 1:
                    cDVE.wait_val(e, cDVE.marks[("avev", pr - 1, nqt - 1)])
                k = 0
                for proj in (0, 1):
                    for t8 in range(8):
                        if k >= 2:
                            e.wait_ge(cDVE.h, base_dve + k - 1)
                        po = 1024 * (t8 % 2)
                        for et in range(8):
                            ins = e.matmul(
                                psS[:, po:po + 512],
                                wp[bufp][:, E * proj + 128 * et:E * proj + 128 * et + 128],
                                xnT[et][:, 512 * t8:512 * t8 + 512],
                                start=(et == 0), stop=(et == 7))
                        cPE.inc(ins)
                        cPE.mark(("qkv", pr, proj, t8))
                        k += 1
                for t32 in range(NT):
                    po = 1024 * (t32 % 2)
                    if k >= 2:
                        e.wait_ge(cDVE.h, base_dve + k - 1)
                    for et in range(8):
                        ins = e.matmul(
                            psS[:, po + 512:po + 640],
                            xnT[et][:, 128 * t32:128 * t32 + 128],
                            wp[bufp][:, 2 * E + 128 * et:2 * E + 128 * et + 128],
                            start=(et == 0), stop=(et == 7))
                    cPE.inc(ins)
                    cPE.mark(("v", pr, t32))
                    k += 1
            b.tensor(a_qkv_pe)

            def a_qkv_dve(e, pr=pr):
                for proj, dst in ((0, QT), (1, KT)):
                    for t8 in range(8):
                        po = 1024 * (t8 % 2)
                        cPE.wait(e, ("qkv", pr, proj, t8))
                        cDVE.inc(e.tensor_copy(dst[:, 512 * t8:512 * t8 + 512],
                                               psS[:, po:po + 512]))
                for t32 in range(NT):
                    po = 1024 * (t32 % 2)
                    cPE.wait(e, ("v", pr, t32))
                    ov = bass.AP(V2.ap().tensor, 130 * t32,
                                 [[4160, 128], [65, 2], [1, 64]])
                    iv = bass.AP(psS.ap().tensor, po + 512,
                                 [[2048, 128], [64, 2], [1, 64]])
                    e.tensor_copy(ov, iv)
                    cDVE.inc(e.tensor_copy(Vt[:, 128 * t32:128 * t32 + 128],
                                           psS[:, po + 512:po + 640]))
                cDVE.mark(("vdone", pr))
            b.vector(a_qkv_dve)


            base_dve2 = cDVE.n

            def a_conv_pe(e, pr=pr, base_dve2=base_dve2):
                cDVE.wait(e, ("vdone", pr))
                for tc in range(4):
                    po = 1024 * (tc % 2)
                    if tc >= 2:
                        e.wait_ge(cDVE.h, base_dve2 + tc - 1)
                    for r in range(32):
                        ins = e.matmul(
                            psS[:, po:po + 512], Vt[:, 128 * r:128 * r + 128],
                            dap(sh, 8192 * tc + 128 * r, [[NSH, 128], [16, 512]]),
                            start=(r == 0), stop=(r == 31))
                    cPE.inc(ins)
                    cPE.mark(("cv", pr, tc))
                for r in range(32):
                    ins = e.matmul(psAV[0:16, 0:128], m16[:],
                                   Vt[:, 128 * r:128 * r + 128],
                                   start=(r == 0), stop=(r == 31))
                cPE.inc(ins)
                cPE.mark(("sv", pr))
            b.tensor(a_conv_pe)

            def a_conv_dve(e, pr=pr):
                for tc in range(4):
                    po = 1024 * (tc % 2)
                    cPE.wait(e, ("cv", pr, tc))
                    cDVE.inc(e.tensor_copy(Gbuf[:, 512 * tc:512 * tc + 512],
                                           psS[:, po:po + 512]))
                cPE.wait(e, ("sv", pr))
                cDVE.inc(e.tensor_copy(svb[:], psAV[0:16, 0:128]))
                cDVE.mark(("sv", pr))
            b.vector(a_conv_dve)

            def a_u_pe(e, pr=pr):
                cDVE.wait(e, ("sv", pr))
                cPE.inc(e.matmul(psAVB[0:1, 0:128], rtcol16[:], svb[:],
                                 start=True, stop=True))
                cPE.mark(("umm", pr))
            b.tensor(a_u_pe)

            def a_u_dve(e, pr=pr):
                cPE.wait(e, ("umm", pr))
                cDVE.inc(e.tensor_copy(urow[:], psAVB[0:1, 0:128]))
                cDVE.mark(("urow", pr))
            b.vector(a_u_dve)

            def a_u_pe2(e, pr=pr):
                cDVE.wait(e, ("urow", pr))
                cPE.inc(e.transpose(psAVB[:, 128:129], urow[:], ident_f[0:1, 0:1]))
                cPE.mark(("utr", pr))
            b.tensor(a_u_pe2)

            def a_u_dve2(e, pr=pr):
                cPE.wait(e, ("utr", pr))
                cDVE.inc(e.tensor_copy(usb[:], psAVB[:, 128:129]))
                cDVE.mark(("u", pr))
            b.vector(a_u_dve2)

            for qt in range(nqt):
                qof = 1024 * qt
                base_exp = cEXP.n

                def a_sc_pe(e, pr=pr, qt=qt, base_exp=base_exp, qof=qof):
                    if qt == 0:
                        cDVE.wait(e, ("u", pr))
                    if (pr, qt) != (0, 0):
                        prev = (("avev", pr, qt - 1) if qt > 0
                                else ("avev", pr - 1, nqt - 1))
                        cDVE.wait_val(e, cDVE.marks[prev])
                    for kc in range(33):
                        if kc < 32:
                            if kc >= 1:
                                e.wait_ge(cEXP.h, base_exp + (kc - 1) + 1)
                            for h2 in range(2):
                                e.matmul(
                                    psS[:, 512 * h2:512 * h2 + 512],
                                    KT[0:64, 128 * kc:128 * kc + 128],
                                    QT[0:64, qof + 512 * h2:qof + 512 * h2 + 512],
                                    start=True, stop=True, tile_position=(0, 0))
                            for h2 in range(2):
                                ins = e.matmul(
                                    psS[:, 1024 + 512 * h2:1024 + 512 * h2 + 512],
                                    KT[64:128, 128 * kc:128 * kc + 128],
                                    QT[64:128, qof + 512 * h2:qof + 512 * h2 + 512],
                                    start=True, stop=True, tile_position=(64, 0))
                            cSC.inc(ins)
                        if kc >= 1:
                            k = kc - 1
                            e.wait_ge(cEXP.h, base_exp + k + 1)
                            for h2 in range(2):
                                e.matmul(psAV[0:65, 512 * h2:512 * h2 + 512],
                                         V2[:, 130 * k:130 * k + 65],
                                         exf[k % 2][:, 512 * h2:512 * h2 + 512],
                                         start=(k == 0), stop=(k == 31),
                                         tile_position=(0, 0))
                            for h2 in range(2):
                                ins = e.matmul(
                                    psAVB[0:65, 512 * h2:512 * h2 + 512],
                                    V2[:, 130 * k + 65:130 * k + 130],
                                    exf[k % 2][:, 1024 + 512 * h2:1024 + 512 * h2 + 512],
                                    start=(k == 0), stop=(k == 31),
                                    tile_position=(0, 0))
                            cAV.inc(ins)
                    cAV.mark(("av", pr, qt))
                b.tensor(a_sc_pe)

                def a_exp(e, pr=pr, qt=qt, base_exp=base_exp):
                    base_sc = cSC.n - 32
                    for k in range(32):
                        e.wait_ge(cSC.h, base_sc + k + 1)
                        cEXP.inc(e.activation(out=exf[k % 2][:], in_=psS[:],
                                              func=AF.Exp, scale=1.0 / 32.0))
                b.scalar(a_exp)

                def a_post_dve(e, pr=pr, qt=qt):
                    cAV.wait(e, ("av", pr, qt))
                    e.reciprocal(rcp[0:1, 0:512], psAV[64:65, 0:512])
                    e.reciprocal(rcp[0:1, 1024:1536], psAV[64:65, 512:1024])
                    e.reciprocal(rcp[0:1, 512:1024], psAVB[64:65, 0:512])
                    cDVE.inc(e.reciprocal(rcp[0:1, 1536:2048], psAVB[64:65, 512:1024]))
                    cDVE.mark(("rcp", pr, qt))
                b.vector(a_post_dve)

                def a_bc_pe(e, pr=pr, qt=qt):
                    cDVE.wait(e, ("rcp", pr, qt))
                    e.matmul(psS[0:64, 0:512], onesrow[0:1, 0:64], rcp[0:1, 0:512],
                             start=True, stop=True, tile_position=(0, 0))
                    e.matmul(psS[64:128, 0:512], onesrow[0:1, 0:64],
                             rcp[0:1, 512:1024],
                             start=True, stop=True, tile_position=(0, 64))
                    e.matmul(psS[0:64, 512:1024], onesrow[0:1, 0:64],
                             rcp[0:1, 1024:1536],
                             start=True, stop=True, tile_position=(0, 0))
                    cPE.inc(e.matmul(psS[64:128, 512:1024], onesrow[0:1, 0:64],
                                     rcp[0:1, 1536:2048], start=True, stop=True,
                                     tile_position=(0, 64)))
                    cPE.mark(("bc", pr, qt))
                b.tensor(a_bc_pe)

                def a_fin_dve(e, pr=pr, qt=qt):
                    cPE.wait(e, ("bc", pr, qt))
                    if (pr, qt) != (0, 0):
                        prev = (("attst", pr, qt - 1) if qt > 0
                                else ("attst", pr - 1, nqt - 1))
                        cGL.wait_val(e, cGL.marks[prev])
                    e.tensor_copy(att32[0:64, :], psAV[0:64, :])
                    e.tensor_copy(att32[64:128, :], psAVB[0:64, :])
                    cDVE.inc(e.tensor_tensor(att32[:], att32[:], psS[:, 0:1024],
                                             ALU.mult))
                    cDVE.mark(("avev", pr, qt))
                    ins_g = None
                    for (hb, m, a_lo, a_n, in_off) in gather_plan(pr, qt):
                        ov = bass.AP(att32.ap().tensor, hb * 1024 + m + 16 * a_lo,
                                     [[1024, 64], [16, a_n]])
                        gv = bass.AP(Gbuf.ap().tensor, hb * 2048 + in_off,
                                     [[2048, 64], [-1, a_n]])
                        ins_g = e.tensor_tensor(ov, ov, gv, ALU.add)
                    if ins_g is not None:
                        fd(e, ins_g)
                    cDVE.inc(e.tensor_scalar(att32[:], att32[:], usb[:], None,
                                             ALU.add))
                    cDVE.mark(("attc", pr, qt))
                b.vector(a_fin_dve)

                def a_st(e, pr=pr, qt=qt, qof=qof):
                    cDVE.wait(e, ("attc", pr, qt))
                    cGL.inc(e.dma_start(
                        out=dap(d["att_hbm"], pr * 128 * S + qof,
                                [[S, 128], [1, 1024]]),
                        in_=att32[:]), 16)
                    cGL.mark(("attst", pr, qt))
                b.gpsimd(a_st)
    st2.close()

    if phases < 4:
        _full_drain()
        ss.close(); sg.close(); return nc
    st3 = ExitStack()
    atl = [st3.enter_context(nc.sbuf_tensor(f"atl{i}", [128, 1024], BF16))
           for i in range(8)]
    sq = [st3.enter_context(nc.sbuf_tensor(f"sq{i}", [128, 1024], BF16))
          for i in range(2)]
    lnst = st3.enter_context(nc.sbuf_tensor("lnst", [1, 3072], F32))
    t1b = st3.enter_context(nc.sbuf_tensor("t1b", [128, 1024], F32))
    stl = st3.enter_context(nc.sbuf_tensor("stl", [128, 1024], BF16))
    sev = [st3.enter_context(nc.sbuf_tensor(f"sev{i}", [128, 1024], BF16))
           for i in range(2)]
    psS3 = st3.enter_context(nc.psum_tensor("psS3", [128, 1024], F32))
    psB1 = st3.enter_context(nc.psum_tensor("psB1", [128, 1024], F32))
    psB2 = st3.enter_context(nc.psum_tensor("psB2", [128, 1024], F32))
    psT3 = st3.enter_context(nc.psum_tensor("psT3", [128, 1024], BF16))

    with nc.Block() as b:
        for qt in range(4):
            qof = 1024 * qt

            def e_ld(e, qt=qt, qof=qof):
                if qt >= 1:
                    cDVE.wait(e, ("sdone", qt - 1))
                for pr2 in range(8):
                    cLD.inc(e.dma_start(
                        out=atl[pr2][:],
                        in_=dap(d["att_hbm"], pr2 * 128 * S + qof,
                                [[S, 128], [1, 1024]])), 16)
                    cLD.mark(("atl", qt, pr2))
            b.sync(e_ld)

            base_pe3 = cPE.n

            def e_sq(e, qt=qt, base_pe3=base_pe3):
                for pr2 in range(8):
                    cLD.wait(e, ("atl", qt, pr2))
                    if pr2 >= 2:
                        e.wait_ge(cPE.h, base_pe3 + pr2 - 1)
                    if qt >= 1:
                        cST.wait(e, ("sst", qt - 1, pr2))
                    cACT.inc(e.activation(out=sq[pr2 % 2][:], in_=atl[pr2][:],
                                          func=AF.Square))
                    cACT.mark(("sq", qt, pr2))
            b.scalar(e_sq)

            def e_stat_pe(e, qt=qt):
                if qt >= 1:
                    cDVE.wait(e, ("var3", qt - 1))
                for pr2 in range(8):
                    cACT.wait(e, ("sq", qt, pr2))
                    for h2 in range(2):
                        e.matmul(psS3[0:1, 512 * h2:512 * h2 + 512], ones_bf[:],
                                 atl[pr2][:, 512 * h2:512 * h2 + 512],
                                 start=(pr2 == 0), stop=(pr2 == 7),
                                 tile_position=(0, 0))
                        ins = e.matmul(psS3[32:33, 512 * h2:512 * h2 + 512],
                                       ones_bf[:],
                                       sq[pr2 % 2][:, 512 * h2:512 * h2 + 512],
                                       start=(pr2 == 0), stop=(pr2 == 7),
                                       tile_position=(0, 32))
                    cPE.inc(ins)
                    cPE.mark(("st3", qt, pr2))
            b.tensor(e_stat_pe)

            def e_stat_dve(e, qt=qt):
                cPE.wait(e, ("st3", qt, 7))
                e.tensor_scalar(lnst[0:1, 0:1024], psS3[0:1, 0:1024], 1.0 / E, None,
                                ALU.mult)
                e.tensor_scalar(lnst[0:1, 1024:2048], psS3[32:33, 0:1024], 1.0 / E,
                                None, ALU.mult)
                e.tensor_tensor(lnst[0:1, 2048:3072], lnst[0:1, 0:1024],
                                lnst[0:1, 0:1024], ALU.mult)
                e.tensor_sub(lnst[0:1, 1024:2048], lnst[0:1, 1024:2048],
                             lnst[0:1, 2048:3072])
                cDVE.inc(e.tensor_scalar(lnst[0:1, 1024:2048], lnst[0:1, 1024:2048],
                                         1e-3, None, ALU.add))
                cDVE.mark(("var3", qt))
            b.vector(e_stat_dve)

            def e_rstd(e, qt=qt):
                cDVE.wait(e, ("var3", qt))
                e.activation(out=lnst[0:1, 2048:3072], in_=lnst[0:1, 1024:2048],
                             func=AF.Ln)
                cACT.inc(e.activation(out=lnst[0:1, 1024:2048],
                                      in_=lnst[0:1, 2048:3072], func=AF.Exp,
                                      scale=-0.5))
                cACT.mark(("rstd3", qt))
            b.scalar(e_rstd)

            def e_ab(e, qt=qt):
                cACT.wait(e, ("rstd3", qt))
                e.tensor_tensor(lnst[0:1, 2048:3072], lnst[0:1, 0:1024],
                                lnst[0:1, 1024:2048], ALU.mult)
                cDVE.inc(e.tensor_scalar(lnst[0:1, 2048:3072],
                                         lnst[0:1, 2048:3072], -1.0, None, ALU.mult))
                cDVE.mark(("ab", qt))
            b.vector(e_ab)

            def e_bc(e, qt=qt):
                cDVE.wait(e, ("ab", qt))
                if qt >= 1:
                    cDVE.wait(e, ("s3", qt - 1, 7))
                for h2 in range(2):
                    e.matmul(psB1[:, 512 * h2:512 * h2 + 512], onesrow[0:1, :],
                             lnst[0:1, 1024 + 512 * h2:1024 + 512 * h2 + 512],
                             start=True, stop=True)
                    ins = e.matmul(psB2[:, 512 * h2:512 * h2 + 512], onesrow[0:1, :],
                                   lnst[0:1, 2048 + 512 * h2:2048 + 512 * h2 + 512],
                                   start=True, stop=True)
                cPE.inc(ins)
                cPE.mark(("bc3", qt))
            b.tensor(e_bc)

            for pr2 in range(8):
                def e_s_dve(e, qt=qt, pr2=pr2):
                    cPE.wait(e, ("bc3", qt))
                    if pr2 >= 1:
                        cPE.wait(e, ("str3", qt, pr2 - 1))
                    e.tensor_tensor(t1b[:], atl[pr2][:], psB1[:], ALU.mult)
                    e.tensor_tensor(t1b[:], t1b[:], psB2[:], ALU.add)
                    cDVE.inc(e.tensor_scalar(stl[:], t1b[:], gammaT[:, pr2:pr2 + 1],
                                             betaT[:, pr2:pr2 + 1], ALU.mult,
                                             op1=ALU.add))
                    cDVE.mark(("s3", qt, pr2))
                b.vector(e_s_dve)

                def e_s_pe(e, qt=qt, pr2=pr2):
                    cDVE.wait(e, ("s3", qt, pr2))
                    if pr2 >= 1:
                        cDVE.wait(e, ("sev", qt, pr2 - 1))
                    for c in range(8):
                        ins = e.transpose(psT3[:, 128 * c:128 * c + 128],
                                          stl[:, 128 * c:128 * c + 128], ident_bf[:])
                    cPE.inc(ins)
                    cPE.mark(("str3", qt, pr2))
                b.tensor(e_s_pe)

                def e_s_ev(e, qt=qt, pr2=pr2):
                    cPE.wait(e, ("str3", qt, pr2))
                    if pr2 >= 2:
                        cST.wait(e, ("sst", qt, pr2 - 2))
                    for c in range(8):
                        ins = e.tensor_copy(sev[pr2 % 2][:, 128 * c:128 * c + 128],
                                            psT3[:, 128 * c:128 * c + 128])
                    cDVE.inc(ins)
                    cDVE.mark(("sev", qt, pr2))
                    if pr2 == 7:
                        cDVE.mark(("sdone", qt))
                b.vector(e_s_ev)

                def e_s_st(e, qt=qt, pr2=pr2, qof=qof):
                    cDVE.wait(e, ("sev", qt, pr2))
                    for c in range(8):
                        cST.inc(e.dma_start(
                            out=dap(d["s_hbm"], (qof + 128 * c) * E + 128 * pr2,
                                    [[E, 128], [1, 128]]),
                            in_=sev[pr2 % 2][:, 128 * c:128 * c + 128]), 16)
                    cST.mark(("sst", qt, pr2))
                b.sync(e_s_st)
    st3.close()

    if phases < 5:
        _full_drain()
        ss.close(); sg.close(); return nc
    st4 = ExitStack()
    xeT = [[st4.enter_context(nc.sbuf_tensor(f"xeT{bq}_{i}", [128, CAP], BF16))
            for i in range(8)] for bq in range(2)]
    w1s = [st4.enter_context(nc.sbuf_tensor(f"w1s{bq}", [128, 4096], BF16))
           for bq in range(2)]
    w3s = [st4.enter_context(nc.sbuf_tensor(f"w3s{bq}", [128, 4096], BF16))
           for bq in range(2)]
    w2s = [st4.enter_context(nc.sbuf_tensor(f"w2s{bq}", [128, 4096], BF16))
           for bq in range(2)]
    h1s = st4.enter_context(nc.sbuf_tensor("h1s", [128, 512], F32))
    hT = [st4.enter_context(nc.sbuf_tensor(f"hT{i}", [128, CAP], BF16))
          for i in range(4)]
    oesb = st4.enter_context(nc.sbuf_tensor("oesb", [128, 1024], BF16))
    psH = [st4.enter_context(nc.psum_tensor(f"psH{i}", [128, 512], F32))
           for i in range(4)]
    psO = [st4.enter_context(nc.psum_tensor(f"psO{i}", [128, 512], F32))
           for i in range(4)]

    with nc.Block() as b:
        for ee in range(NEXP):
            bq = ee % 2

            def m_w(e, ee=ee, bq=bq):
                if ee >= 2:
                    cPE.wait(e, ("mm13", ee - 2))
                    cPE.wait(e, ("mm2", ee - 2))
                cW.inc(e.dma_start(
                    out=w1s[bq][:],
                    in_=dap(d["w1"], ee * E * HEXP,
                            [[HEXP, 128], [128 * HEXP, 8], [1, HEXP]])), 16)
                cW.inc(e.dma_start(
                    out=w3s[bq][:],
                    in_=dap(d["w3"], ee * E * HEXP,
                            [[HEXP, 128], [128 * HEXP, 8], [1, HEXP]])), 16)
                cW.inc(e.dma_start(
                    out=w2s[bq][:],
                    in_=dap(d["w2"], ee * HEXP * E,
                            [[E, 128], [128 * E, 4], [1, E]])), 16)
                cW.mark(("mw", ee))
            b.gpsimd(m_w)

            def m_xe(e, ee=ee, bq=bq):
                if ee >= 2:
                    cPE.wait(e, ("mm13", ee - 2))
                for i in range(8):
                    cXE.inc(e.dma_start(
                        out=xeT[bq][i][:],
                        in_=d["xe_hbm"].ap()[CAP * ee:CAP * (ee + 1),
                                             128 * i:128 * (i + 1)],
                        transpose=True), 16)
                cXE.mark(("xe", ee))
            b.scalar(m_xe)

            base_act4 = cACT.n
            base_dve4 = cDVE.n

            def m_pe1(e, ee=ee, bq=bq, base_act4=base_act4, base_dve4=base_dve4):
                cW.wait(e, ("mw", ee))
                cXE.wait(e, ("xe", ee))
                k = 0
                for ht in range(4):
                    for stq in range(2):
                        if k >= 2:
                            e.wait_ge(cACT.h, base_act4 + k - 1)
                        for et in range(8):
                            ins = e.matmul(
                                psH[(2 * k) % 4][:],
                                w1s[bq][:, 512 * et + 128 * ht:512 * et + 128 * ht + 128],
                                xeT[bq][et][:, 512 * stq:512 * stq + 512],
                                start=(et == 0), stop=(et == 7))
                        cPE.inc(ins)
                        cPE.mark(("h1", ee, ht, stq))
                        if k >= 2:
                            e.wait_ge(cDVE.h, base_dve4 + k - 1)
                        for et in range(8):
                            ins = e.matmul(
                                psH[(2 * k + 1) % 4][:],
                                w3s[bq][:, 512 * et + 128 * ht:512 * et + 128 * ht + 128],
                                xeT[bq][et][:, 512 * stq:512 * stq + 512],
                                start=(et == 0), stop=(et == 7))
                        cPE.inc(ins)
                        cPE.mark(("h3", ee, ht, stq))
                        k += 1
                cPE.mark(("mm13", ee))
            b.tensor(m_pe1)

            def m_silu(e, ee=ee, base_dve4=base_dve4):
                for k in range(8):
                    ht, stq = divmod(k, 2)
                    cPE.wait(e, ("h1", ee, ht, stq))
                    if k >= 1:
                        e.wait_ge(cDVE.h, base_dve4 + k)
                    cACT.inc(e.activation(out=h1s[:], in_=psH[(2 * k) % 4][:],
                                          func=AF.Silu))
                    cACT.mark(("sl", ee, k))
            b.scalar(m_silu)

            def m_hT(e, ee=ee):
                for k in range(8):
                    ht, stq = divmod(k, 2)
                    cACT.wait(e, ("sl", ee, k))
                    cPE.wait(e, ("h3", ee, ht, stq))
                    cDVE.inc(e.tensor_tensor(hT[ht][:, 512 * stq:512 * stq + 512],
                                             h1s[:], psH[(2 * k + 1) % 4][:],
                                             ALU.mult))
                cDVE.mark(("hT", ee))
            b.vector(m_hT)

            base_dve5 = cDVE.n

            def m_pe2(e, ee=ee, bq=bq, base_dve5=base_dve5):
                cDVE.wait(e, ("hT", ee))
                for st8 in range(8):
                    for dt in range(2):
                        j = 2 * st8 + dt
                        if j >= 4:
                            e.wait_ge(cDVE.h, base_dve5 + j - 3)
                        for ht in range(4):
                            ins = e.matmul(
                                psO[j % 4][:],
                                hT[ht][:, 128 * st8:128 * st8 + 128],
                                w2s[bq][:, 1024 * ht + 512 * dt:1024 * ht + 512 * dt + 512],
                                start=(ht == 0), stop=(ht == 3))
                        cPE.inc(ins)
                        cPE.mark(("o", ee, st8, dt))
                cPE.mark(("mm2", ee))
            b.tensor(m_pe2)

            base_st4 = cST.n

            def m_oev(e, ee=ee, base_st4=base_st4):
                for st8 in range(8):
                    if st8 >= 1:
                        e.wait_ge(cST.h, base_st4 + 16 * st8)
                    for dt in range(2):
                        cPE.wait(e, ("o", ee, st8, dt))
                        cDVE.inc(e.tensor_copy(oesb[:, 512 * dt:512 * dt + 512],
                                               psO[(2 * st8 + dt) % 4][:]))
                    cDVE.mark(("orow", ee, st8))
            b.vector(m_oev)

            def m_ost(e, ee=ee):
                for st8 in range(8):
                    cDVE.wait(e, ("orow", ee, st8))
                    cST.inc(e.dma_start(
                        out=d["oute_hbm"].ap()[CAP * ee + 128 * st8:
                                               CAP * ee + 128 * (st8 + 1), :],
                        in_=oesb[:]), 16)
                    cST.mark(("ost", ee, st8))
            b.sync(m_ost)
    st4.close()

    if phases < 6:
        _full_drain()
        ss.close(); sg.close(); return nc
    st5 = ExitStack()
    ws1s = st5.enter_context(nc.sbuf_tensor("ws1s", [128, 8192], BF16))
    ws3s = st5.enter_context(nc.sbuf_tensor("ws3s", [128, 8192], BF16))
    ws2s = st5.enter_context(nc.sbuf_tensor("ws2s", [128, 8192], BF16))
    hs1 = st5.enter_context(nc.sbuf_tensor("hs1", [128, 512], F32))
    hsT = [st5.enter_context(nc.sbuf_tensor(f"hsT{i}", [128, 512], BF16))
           for i in range(8)]
    yg = [st5.enter_context(nc.sbuf_tensor(f"yg{i}", [128, 1024], F32))
          for i in range(4)]
    ysb = st5.enter_context(nc.sbuf_tensor("ysb", [128, 1024], F32))
    gsb = st5.enter_context(nc.sbuf_tensor("gsb", [128, 1024], BF16))
    ssb = [st5.enter_context(nc.sbuf_tensor(f"ssb{i}", [128, 1024], BF16))
           for i in range(2)]
    xsb = [st5.enter_context(nc.sbuf_tensor(f"xsb{i}", [128, 1024], F32))
           for i in range(2)]
    osb = [st5.enter_context(nc.sbuf_tensor(f"osb{i}", [128, 1024], F32))
           for i in range(2)]
    psQ = [st5.enter_context(nc.psum_tensor(f"psQ{i}", [128, 512], F32))
           for i in range(4)]
    psZ = [st5.enter_context(nc.psum_tensor(f"psZ{i}", [128, 512], F32))
           for i in range(2)]

    with nc.Block() as b:
        def f_w(e):
            for nm, t in (("ws1", ws1s), ("ws3", ws3s), ("ws2", ws2s)):
                cW.inc(e.dma_start(
                    out=t[:],
                    in_=dap(d[nm], 0, [[E, 128], [128 * E, 8], [1, E]])), 16)
            cW.mark("wsh")
        b.gpsimd(f_w)

        for tt in range(8):
            tof = 512 * tt
            base_dve6 = cDVE.n

            def f_pe1(e, tt=tt, tof=tof, base_dve6=base_dve6):
                cW.wait(e, "wsh")
                for hst in range(8):
                    if hst >= 2:
                        e.wait_ge(cDVE.h, base_dve6 + hst - 1)
                    for et in range(8):
                        e.matmul(psQ[(2 * hst) % 4][:],
                                 ws1s[:, 1024 * et + 128 * hst:1024 * et + 128 * hst + 128],
                                 xnT[et][:, tof:tof + 512],
                                 start=(et == 0), stop=(et == 7))
                    ins = None
                    for et in range(8):
                        ins = e.matmul(psQ[(2 * hst + 1) % 4][:],
                                       ws3s[:, 1024 * et + 128 * hst:1024 * et + 128 * hst + 128],
                                       xnT[et][:, tof:tof + 512],
                                       start=(et == 0), stop=(et == 7))
                    cPE.inc(ins)
                    cPE.mark(("hs", tt, hst))
            if ks5 >= 2: b.tensor(f_pe1)

            def f_silu(e, tt=tt, base_dve6=base_dve6):
                for hst in range(8):
                    cPE.wait(e, ("hs", tt, hst))
                    if hst >= 1:
                        e.wait_ge(cDVE.h, base_dve6 + hst)
                    cACT.inc(e.activation(out=hs1[:], in_=psQ[(2 * hst) % 4][:],
                                          func=AF.Silu))
                    cACT.mark(("hsl", tt, hst))
            if ks5 >= 2: b.scalar(f_silu)

            def f_hs(e, tt=tt):
                for hst in range(8):
                    cACT.wait(e, ("hsl", tt, hst))
                    cDVE.inc(e.tensor_tensor(hsT[hst][:], hs1[:],
                                             psQ[(2 * hst + 1) % 4][:], ALU.mult))
                cDVE.mark(("hsd", tt))
            if ks5 >= 2: b.vector(f_hs)

            for tsub in range(4):
                ttg = 4 * tt + tsub

                def f_gthr(e, ttg=ttg):
                    if ttg >= 1 and ks5 >= 6:
                        cDVE.wait(e, ("yw", ttg - 1))
                    for r in range(4):
                        cGL.inc(e.indirect_dma_start(
                            yg[r][:], None, d["oute_hbm"].ap()[:],
                            IndirectOffsetOnAxis(
                                slotbuf[:, 4 * ttg + r:4 * ttg + r + 1], 0)), 16)
                    cGL.mark(("yg", ttg))
                if ks5 >= 3: b.gpsimd(f_gthr)

                def f_ld(e, ttg=ttg):
                    if ttg >= 2 and ks5 >= 6:
                        cDVE.wait(e, ("sgm", ttg - 2))
                    cLD.inc(e.dma_start(
                        out=ssb[ttg % 2][:],
                        in_=d["s_hbm"].ap()[128 * ttg:128 * (ttg + 1), :]), 16)
                    cLD.inc(e.dma_start(
                        out=xsb[ttg % 2][:],
                        in_=d["x"].ap()[128 * ttg:128 * (ttg + 1), :]), 16)
                    cLD.mark(("sx", ttg))
                if ks5 >= 4: b.sync(f_ld)

                def f_pe2(e, tt=tt, tsub=tsub, ttg=ttg):
                    cDVE.wait(e, ("hsd", tt))
                    if ttg >= 1 and ks5 >= 6:
                        cDVE.wait(e, ("zc", ttg - 1))
                    for dt in range(2):
                        for hst in range(8):
                            ins = e.matmul(
                                psZ[dt][:],
                                hsT[hst][:, 128 * tsub:128 * tsub + 128],
                                ws2s[:, 1024 * hst + 512 * dt:1024 * hst + 512 * dt + 512],
                                start=(hst == 0), stop=(hst == 7))
                    cPE.inc(ins)
                    cPE.mark(("z", ttg))
                if ks5 >= 5: b.tensor(f_pe2)

                def f_comb(e, ttg=ttg):
                    if ks5 >= 3:
                        cGL.wait(e, ("yg", ttg))
                    if ttg >= 1:
                        cACT.wait(e, ("g", ttg - 1))
                    e.tensor_scalar(ysb[:], yg[0][:], w4buf[:, 4 * ttg:4 * ttg + 1],
                                    None, ALU.mult)
                    for r in range(1, 4):
                        e.scalar_tensor_tensor(
                            ysb[:], yg[r][:],
                            w4buf[:, 4 * ttg + r:4 * ttg + r + 1], ysb[:],
                            ALU.mult, ALU.add)
                    cPE.wait(e, ("z", ttg))
                    e.tensor_tensor(ysb[:, 0:512], ysb[:, 0:512], psZ[0][:], ALU.add)
                    cDVE.inc(e.tensor_tensor(ysb[:, 512:1024], ysb[:, 512:1024],
                                             psZ[1][:], ALU.add))
                    cDVE.mark(("zc", ttg))
                    cDVE.mark(("yw", ttg))
                if ks5 >= 6: b.vector(f_comb)

                def f_g(e, ttg=ttg):
                    cDVE.wait(e, ("zc", ttg))
                    cACT.inc(e.activation(out=gsb[:], in_=ysb[:], func=AF.Silu))
                    cACT.mark(("g", ttg))
                if ks5 >= 6: b.scalar(f_g)

                def f_out(e, ttg=ttg):
                    cACT.wait(e, ("g", ttg))
                    if ks5 >= 4:
                        cLD.wait(e, ("sx", ttg))
                    if ttg >= 2:
                        cOUT.wait(e, ("ostc", ttg - 2))
                    e.tensor_tensor(osb[ttg % 2][:], ssb[ttg % 2][:], gsb[:], ALU.mult)
                    cDVE.inc(e.tensor_tensor(osb[ttg % 2][:], osb[ttg % 2][:],
                                             xsb[ttg % 2][:], ALU.add))
                    cDVE.mark(("o5", ttg))
                    cDVE.mark(("sgm", ttg))
                if ks5 >= 6: b.vector(f_out)

                def f_ost(e, ttg=ttg):
                    cDVE.wait(e, ("o5", ttg))
                    cOUT.inc(e.dma_start(
                        out=d["out"].ap()[128 * ttg:128 * (ttg + 1), :],
                        in_=osb[ttg % 2][:]), 16)
                    cOUT.mark(("ostc", ttg))
                if ks5 >= 6: b.sync(f_ost)

        def f_fin(e):
            e.wait_ge(cOUT.h, cOUT.n)
        b.sync(f_fin)
    st5.close()
    _full_drain()
    ss.close()
    sg.close()
    return nc


def _consts():
    c = {}
    c["ident_bf"] = np.eye(128, dtype=np.float32).astype(BF)
    c["ident_f32"] = np.eye(128, dtype=np.float32)
    c["lstr_f32"] = np.triu(np.ones((128, 128), np.float32), 1)
    c["ones_bf"] = np.ones((128, 1), np.float32).astype(BF)
    c["ones_f32"] = np.ones((128, 1), np.float32)
    c["onesrow_f32"] = np.ones((1, 128), np.float32)
    c["m16_bf"] = (np.arange(128)[:, None] % 16 ==
                   np.arange(16)[None, :]).astype(np.float32).astype(BF)
    c["e1024"] = (1024.0 * np.arange(32, dtype=np.float32)).reshape(1, 32)
    return c


def kernel(L0_attn_x, L0_attn_Wq, L0_attn_Wk, L0_attn_Wv, L0_attn_rel_table,
           L0_attn_gamma, L0_attn_beta, L0_ffn_Wg, L0_ffn_W1, L0_ffn_W2,
           L0_ffn_W3, L0_ffn_Ws1, L0_ffn_Ws2, L0_ffn_Ws3):
    nc = build_nc()
    c = _consts()
    shared = dict(
        wq=np.ascontiguousarray(L0_attn_Wq), wk=np.ascontiguousarray(L0_attn_Wk),
        wv=np.ascontiguousarray(L0_attn_Wv),
        rel=np.ascontiguousarray(L0_attn_rel_table),
        gamma_r=np.ascontiguousarray(np.asarray(L0_attn_gamma).reshape(8, 128)),
        beta_r=np.ascontiguousarray(np.asarray(L0_attn_beta).reshape(8, 128)),
        wg=np.ascontiguousarray(L0_ffn_Wg),
        w1=np.ascontiguousarray(L0_ffn_W1), w2=np.ascontiguousarray(L0_ffn_W2),
        w3=np.ascontiguousarray(L0_ffn_W3),
        ws1=np.ascontiguousarray(L0_ffn_Ws1), ws2=np.ascontiguousarray(L0_ffn_Ws2),
        ws3=np.ascontiguousarray(L0_ffn_Ws3),
        **c)
    in_maps = []
    for bi in range(8):
        m = dict(shared)
        m["x"] = np.ascontiguousarray(L0_attn_x[bi])
        in_maps.append(m)
    res = run_bass_kernel_spmd(nc, in_maps, core_ids=list(range(8)))
    return np.stack([np.asarray(r["out"]) for r in res.results], axis=0)


# version 06 (final)
# speedup vs baseline: 1.0774x; 1.0774x over previous
import os
import numpy as np
import ml_dtypes
import concourse.bass as bass
import concourse.mybir as mybir
from concourse.bass_core_rust import IndirectOffsetOnAxis
from concourse.bass_utils import run_bass_kernel_spmd
from contextlib import ExitStack

F32 = mybir.dt.float32
BF16 = mybir.dt.bfloat16
U32 = mybir.dt.uint32
AF = mybir.ActivationFunctionType
ALU = mybir.AluOpType
AX = mybir.AxisListType.X
BF = ml_dtypes.bfloat16

S = 4096
E = 1024
NP = 8
NEXP = 32
HEXP = 512
CAP = 1024
NSH = 36736
DFLAT = NSH + 128
NT = 32


class Ch:
    def __init__(self, nc, st, name):
        self.h = st.enter_context(nc.semaphore(name))
        self.n = 0
        self.marks = {}

    def inc(self, instr, k=1):
        instr.then_inc(self.h, k)
        self.n += k
        return instr

    def mark(self, key):
        self.marks[key] = self.n

    def wait(self, eng, key=None):
        v = self.n if key is None else self.marks[key]
        if v > 0:
            eng.wait_ge(self.h, v)

    def wait_val(self, eng, v):
        if v > 0:
            eng.wait_ge(self.h, v)


def dap(t, off, dims):
    return bass.AP(t.ap().tensor, off, dims)


def build_nc(phases=6):
    ksub = int(os.environ.get("KSUB", "99"))
    ks5 = int(os.environ.get("KS5", "99"))
    npr = int(os.environ.get("NPR", "8"))
    nqt = int(os.environ.get("NQT", "8"))
    nc = bass.Bass("TRN2", target_bir_lowering=False)
    import functools
    _orig_mm = type(nc.tensor).matmul

    def _mm(self, *a, **kw):
        kw.setdefault("skip_group_check", True)
        return _orig_mm(self, *a, **kw)

    nc.tensor.matmul = functools.partial(_mm, nc.tensor)

    d = {}
    d["x"] = nc.dram_tensor("x", [S, E], F32, kind="ExternalInput")
    for nm, shp in [("wq", [E, E]), ("wk", [E, E]), ("wv", [E, E]),
                    ("rel", [2047, 16]), ("gamma_r", [8, 128]), ("beta_r", [8, 128]),
                    ("wg", [E, NEXP]),
                    ("w1", [NEXP, E, HEXP]), ("w2", [NEXP, HEXP, E]),
                    ("w3", [NEXP, E, HEXP]),
                    ("ws1", [E, E]), ("ws2", [E, E]), ("ws3", [E, E])]:
        d[nm] = nc.dram_tensor(nm, shp, F32, kind="ExternalInput")
    d["ident_bf"] = nc.dram_tensor("ident_bf", [128, 128], BF16, kind="ExternalInput")
    d["ident_f32"] = nc.dram_tensor("ident_f32", [128, 128], F32, kind="ExternalInput")
    d["lstr_f32"] = nc.dram_tensor("lstr_f32", [128, 128], F32, kind="ExternalInput")
    d["ones_bf"] = nc.dram_tensor("ones_bf", [128, 1], BF16, kind="ExternalInput")
    d["ones_f32"] = nc.dram_tensor("ones_f32", [128, 1], F32, kind="ExternalInput")
    d["onesrow_f32"] = nc.dram_tensor("onesrow_f32", [1, 128], F32, kind="ExternalInput")
    d["m16_bf"] = nc.dram_tensor("m16_bf", [128, 16], BF16, kind="ExternalInput")
    d["e1024"] = nc.dram_tensor("e1024", [1, 32], F32, kind="ExternalInput")
    d["xn_hbm"] = nc.dram_tensor("xn_hbm", [S, E], BF16, kind="Internal")
    d["att_hbm"] = nc.dram_tensor("att_hbm", [NP, 128, S], BF16, kind="Internal")
    d["s_hbm"] = nc.dram_tensor("s_hbm", [S, E], BF16, kind="Internal")
    d["dflat"] = nc.dram_tensor("dflat", [DFLAT], BF16, kind="Internal")
    d["xe_hbm"] = nc.dram_tensor("xe_hbm", [NEXP * CAP, E], BF16, kind="Internal")
    d["oute_hbm"] = nc.dram_tensor("oute_hbm", [NEXP * CAP, E], BF16, kind="Internal")
    d["out"] = nc.dram_tensor("out", [S, E], F32, kind="ExternalOutput")

    sg = ExitStack()
    ss = ExitStack()

    xnT = [sg.enter_context(nc.sbuf_tensor(f"xnT{i}", [128, S], BF16)) for i in range(8)]
    ident_bf = sg.enter_context(nc.sbuf_tensor("identb", [128, 128], BF16))
    ident_f = sg.enter_context(nc.sbuf_tensor("identf", [128, 128], F32))
    lstr = sg.enter_context(nc.sbuf_tensor("lstr", [128, 128], F32))
    ones_bf = sg.enter_context(nc.sbuf_tensor("onesb", [128, 1], BF16))
    ones_f = sg.enter_context(nc.sbuf_tensor("onesf", [128, 1], F32))
    onesrow = sg.enter_context(nc.sbuf_tensor("onesrow", [1, 128], F32))
    m16 = sg.enter_context(nc.sbuf_tensor("m16", [128, 16], BF16))
    e1024 = sg.enter_context(nc.sbuf_tensor("e1024s", [1, 32], F32))
    rtcol16 = sg.enter_context(nc.sbuf_tensor("rtcol16", [16, 1], F32))
    gammaT = sg.enter_context(nc.sbuf_tensor("gammaT", [128, 8], F32))
    betaT = sg.enter_context(nc.sbuf_tensor("betaT", [128, 8], F32))
    slotbuf = sg.enter_context(nc.sbuf_tensor("slotbuf", [128, 128], U32))
    w4buf = sg.enter_context(nc.sbuf_tensor("w4buf", [128, 128], F32))

    cLD = Ch(nc, ss, "cLD")
    cGL = Ch(nc, ss, "cGL")
    cPE = Ch(nc, ss, "cPE")
    cACT = Ch(nc, ss, "cACT")
    cDVE = Ch(nc, ss, "cDVE")
    cST = Ch(nc, ss, "cST")
    cXE = Ch(nc, ss, "cXE")
    cEXP = Ch(nc, ss, "cEXP")
    cSC = Ch(nc, ss, "cSC")
    cAV = Ch(nc, ss, "cAV")
    cW = Ch(nc, ss, "cW")
    cOUT = Ch(nc, ss, "cOUT")
    cFD = Ch(nc, ss, "cFD")
    cFA = Ch(nc, ss, "cFA")

    def fd(e, instr):
        cFD.inc(instr)
        e.wait_ge(cFD.h, cFD.n)

    def fa(e, instr):
        cFA.inc(instr)
        e.wait_ge(cFA.h, cFA.n)


    def _drain(nc_, chans_by_eng):
        with nc_.Block() as bdr:
            for engname, chans in chans_by_eng.items():
                def f(e, chans=chans):
                    for ch in chans:
                        if ch.n > 0:
                            e.wait_ge(ch.h, ch.n)
                getattr(bdr, engname)(f)

    def _full_drain():
        _drain(nc, {"sync": [cLD, cST, cOUT], "gpsimd": [cGL, cW],
                    "scalar": [cXE]})

    st0 = ExitStack()
    xt = [st0.enter_context(nc.sbuf_tensor(f"xt{i}", [128, E], F32)) for i in range(2)]
    xnt = [st0.enter_context(nc.sbuf_tensor(f"xnt{i}", [128, E], BF16)) for i in range(2)]
    scrap = st0.enter_context(nc.sbuf_tensor("scrap", [128, E], F32))
    stat = st0.enter_context(nc.sbuf_tensor("stat", [128, 8], F32))
    rta = st0.enter_context(nc.sbuf_tensor("rta", [128, 256], F32))
    rtd = st0.enter_context(nc.sbuf_tensor("rtd", [128, 256], BF16))
    rtrow = st0.enter_context(nc.sbuf_tensor("rtrow", [1, 16], F32))
    rtb = st0.enter_context(nc.sbuf_tensor("rtb", [128, 16], F32))
    rtb256 = st0.enter_context(nc.sbuf_tensor("rtb256", [128, 256], F32))
    gbl = st0.enter_context(nc.sbuf_tensor("gbl", [128, 256], F32))
    zt = st0.enter_context(nc.sbuf_tensor("zt", [128, 4096], BF16))
    ps0 = st0.enter_context(nc.psum_tensor("ps0", [128, 512], F32))
    psT0 = st0.enter_context(nc.psum_tensor("psT0", [128, 1024], BF16))

    with nc.Block() as b:
        def z_init(e):
            cDVE.inc(e.memset(rta[:], 0.0))
            cDVE.mark("rta0")
            cDVE.inc(e.memset(zt[:], 0.0))
            cDVE.mark("zt")
        b.vector(z_init)

        def ld_consts(e):
            for t, src in [(ident_bf, d["ident_bf"]), (ident_f, d["ident_f32"]),
                           (lstr, d["lstr_f32"]), (ones_bf, d["ones_bf"]),
                           (ones_f, d["ones_f32"]), (onesrow, d["onesrow_f32"]),
                           (m16, d["m16_bf"]), (e1024, d["e1024"])]:
                cLD.inc(e.dma_start(out=t[:], in_=src.ap()[:]), 16)
            cLD.inc(e.dma_start(out=rtcol16[:],
                                in_=dap(d["rel"], 2046 * 16, [[1, 16], [1, 1]])), 16)
            cLD.inc(e.dma_start(out=rtrow[:],
                                in_=dap(d["rel"], 2046 * 16, [[16, 1], [1, 16]])), 16)
            cDVE.wait(e, "rta0")
            cLD.inc(e.dma_start(out=rta[0:127, :],
                                in_=dap(d["rel"], 0, [[256, 127], [1, 256]])), 16)
            cLD.inc(e.dma_start(out=rta[127:128, 0:240],
                                in_=dap(d["rel"], 127 * 256, [[240, 1], [1, 240]])), 16)
            cLD.inc(e.dma_start(out=gbl[0:8, 0:128], in_=d["gamma_r"].ap()[:]), 16)
            cLD.inc(e.dma_start(out=gbl[0:8, 128:256], in_=d["beta_r"].ap()[:]), 16)
            cLD.mark("consts")
        b.sync(ld_consts)

        if ksub < 2:
            _noop = True
        def pre_pe(e):
            cLD.wait(e, "consts")
            e.transpose(ps0[:, 0:8], gbl[0:8, 0:128], ident_f[0:8, 0:8])
            cPE.inc(e.transpose(ps0[:, 8:16], gbl[0:8, 128:256], ident_f[0:8, 0:8]))
            cPE.mark("gbT")
            cPE.inc(e.matmul(ps0[:, 256:272], onesrow[0:1, :], rtrow[:],
                             start=True, stop=True))
            cPE.mark("rtb")
        if ksub >= 2: b.tensor(pre_pe)

        def pre_dve(e):
            cPE.wait(e, "rtb")
            e.tensor_copy(gammaT[:], ps0[:, 0:8])
            e.tensor_copy(betaT[:], ps0[:, 8:16])
            fd(e, e.tensor_copy(rtb[:], ps0[:, 256:272]))
            for u in range(16):
                ins = e.tensor_copy(rtb256[:, 16 * u:16 * u + 16], rtb[:])
            fd(e, ins)
            cDVE.inc(e.tensor_sub(rtd[:], rta[:], rtb256[:]))
            cDVE.mark("rtd")
        if ksub >= 21: b.vector(pre_dve)

        def pre_st(e):
            cDVE.wait(e, "rtd")
            cST.inc(e.dma_start(out=dap(d["dflat"], 0, [[256, 127], [1, 256]]),
                                in_=rtd[0:127, :]), 16)
            cST.inc(e.dma_start(out=dap(d["dflat"], 127 * 256, [[240, 1], [1, 240]]),
                                in_=rtd[127:128, 0:240]), 16)
            cDVE.wait(e, "zt")
            nz = (DFLAT - 32752 + 1) // 2
            cST.inc(e.dma_start(out=dap(d["dflat"], 32752, [[nz, 2], [1, nz]]),
                                in_=zt[0:2, 0:nz]), 16)
            cST.mark("dflat")
        if ksub >= 22: b.sync(pre_st)

        def zero_fill(e):
            cDVE.wait(e, "zt")
            for ch in range(64):
                cGL.inc(e.dma_start(out=dap(d["xe_hbm"], ch * 128 * 4096,
                                            [[4096, 128], [1, 4096]]),
                                    in_=zt[:]), 16)
            cGL.mark("xezero")
        if ksub >= 23: b.gpsimd(zero_fill)

        for tt in range(NT):
            def ln_sync(e, tt=tt):
                if tt >= 2:
                    cDVE.wait(e, ("xnv", tt - 2))
                cLD.inc(e.dma_start(out=xt[tt % 2][:],
                                    in_=d["x"].ap()[128 * tt:128 * (tt + 1), :]), 16)
                cLD.mark(("x", tt))
            if ksub >= 24: b.sync(ln_sync)

            def ln_v1(e, tt=tt):
                cLD.wait(e, ("x", tt))
                e.scalar_tensor_tensor(scrap[:], xt[tt % 2][:], 1.0, xt[tt % 2][:],
                                       ALU.mult, ALU.mult, accum_out=stat[:, 0:1])
                e.reduce_sum(stat[:, 1:2], xt[tt % 2][:], axis=AX)
                e.tensor_scalar(stat[:, 2:3], stat[:, 0:1], 1.0 / E, 1e-5,
                                ALU.mult, op1=ALU.add)
                e.tensor_tensor(stat[:, 5:6], stat[:, 1:2], stat[:, 1:2], ALU.mult)
                e.tensor_scalar(stat[:, 6:7], stat[:, 1:2], -1.0 / E, None, ALU.mult)
                cDVE.inc(e.scalar_tensor_tensor(stat[:, 2:3], stat[:, 5:6],
                                                -1.0 / (E * E), stat[:, 2:3],
                                                ALU.mult, ALU.add))
                cDVE.mark(("var", tt))
            if ksub >= 24: b.vector(ln_v1)

            def ln_a(e, tt=tt):
                cDVE.wait(e, ("var", tt))
                fa(e, e.activation(out=stat[:, 4:5], in_=stat[:, 2:3], func=AF.Ln))
                cACT.inc(e.activation(out=stat[:, 3:4], in_=stat[:, 4:5], func=AF.Exp,
                                      scale=-0.5))
                cACT.mark(("rstd", tt))
            if ksub >= 24: b.scalar(ln_a)

            def ln_v2(e, tt=tt):
                cACT.wait(e, ("rstd", tt))
                if tt >= 2:
                    cST.wait(e, ("xnst", tt - 2))
                    cPE.wait(e, ("xntr", tt - 2))
                cDVE.inc(e.tensor_scalar(xnt[tt % 2][:], xt[tt % 2][:], stat[:, 6:7],
                                         stat[:, 3:4], ALU.add, op1=ALU.mult))
                cDVE.mark(("xnv", tt))
            if ksub >= 24: b.vector(ln_v2)

            def ln_pe(e, tt=tt):
                cDVE.wait(e, ("xnv", tt))
                if tt >= 1:
                    cDVE.wait(e, ("xnev", tt - 1))
                for c in range(8):
                    ins = e.transpose(psT0[:, 128 * c:128 * c + 128],
                                      xnt[tt % 2][:, 128 * c:128 * c + 128], ident_bf[:])
                cPE.inc(ins)
                cPE.mark(("xntr", tt))
            if ksub >= 24: b.tensor(ln_pe)

            def ln_v3(e, tt=tt):
                cPE.wait(e, ("xntr", tt))
                for c in range(8):
                    ins = e.tensor_copy(xnT[c][:, 128 * tt:128 * tt + 128],
                                        psT0[:, 128 * c:128 * c + 128])
                cDVE.inc(ins)
                cDVE.mark(("xnev", tt))
            if ksub >= 24: b.vector(ln_v3)

            def ln_st(e, tt=tt):
                cDVE.wait(e, ("xnv", tt))
                cST.inc(e.dma_start(out=d["xn_hbm"].ap()[128 * tt:128 * (tt + 1), :],
                                    in_=xnt[tt % 2][:]), 16)
                cST.mark(("xnst", tt))
            if ksub >= 24: b.sync(ln_st)
    st0.close()

    if phases < 2:
        _full_drain()
        ss.close(); sg.close(); return nc
    st1 = ExitStack()
    wg_sb = st1.enter_context(nc.sbuf_tensor("wg_sb", [128, 256], BF16))
    probs = st1.enter_context(nc.sbuf_tensor("probs", [128, 32], F32))
    tops = st1.enter_context(nc.sbuf_tensor("tops", [128, 8], F32))
    oh = st1.enter_context(nc.sbuf_tensor("oh", [128, 32], F32))
    selb = st1.enter_context(nc.sbuf_tensor("selb", [128, 32], F32))
    rsm = st1.enter_context(nc.sbuf_tensor("rsm", [128, 4], F32))
    badd = st1.enter_context(nc.sbuf_tensor("badd", [1, 32], F32))
    xnr = [st1.enter_context(nc.sbuf_tensor(f"xnr{i}", [128, E], BF16)) for i in range(2)]
    psL = st1.enter_context(nc.psum_tensor("psL", [128, 512], F32))
    psRr = st1.enter_context(nc.psum_tensor("psRr", [128, 512], F32))
    psCr = st1.enter_context(nc.psum_tensor("psCr", [128, 512], F32))

    with nc.Block() as b:
        def r_init_g(e):
            cGL.inc(e.dma_start(
                out=wg_sb[:],
                in_=dap(d["wg"], 0, [[32, 128], [128 * 32, 8], [1, 32]])), 16)
            cGL.mark("wg")
        b.gpsimd(r_init_g)

        def r_init_v(e):
            cDVE.inc(e.tensor_copy(badd[:], e1024[:]))
            cDVE.mark("badd0")
        b.vector(r_init_v)

        for tt in range(NT):
            def r_sync(e, tt=tt):
                if tt >= 2:
                    cGL.wait(e, ("scat", tt - 2))
                cLD.inc(e.dma_start(out=xnr[tt % 2][:],
                                    in_=d["xn_hbm"].ap()[128 * tt:128 * (tt + 1), :]),
                        16)
                cLD.mark(("xnr", tt))
            b.sync(r_sync)

            def r_pe1(e, tt=tt):
                cGL.wait(e, "wg")
                if tt >= 1:
                    cACT.wait(e, ("pexp", tt - 1))
                for et in range(8):
                    ins = e.matmul(psL[:, 0:32], xnT[et][:, 128 * tt:128 * tt + 128],
                                   wg_sb[:, 32 * et:32 * et + 32],
                                   start=(et == 0), stop=(et == 7))
                cPE.inc(ins)
                cPE.mark(("lg", tt))
            b.tensor(r_pe1)

            def r_v1(e, tt=tt):
                cPE.wait(e, ("lg", tt))
                cDVE.inc(e.reduce_max(rsm[:, 0:1], psL[:, 0:32], axis=AX, negate=True))
                cDVE.mark(("negm", tt))
            b.vector(r_v1)

            def r_a1(e, tt=tt):
                cDVE.wait(e, ("negm", tt))
                cACT.inc(e.activation(out=probs[:], in_=psL[:, 0:32], func=AF.Exp,
                                      bias=rsm[:, 0:1]))
                cACT.mark(("pexp", tt))
            b.scalar(r_a1)

            def r_v2(e, tt=tt):
                cACT.wait(e, ("pexp", tt))
                fd(e, e.reduce_sum(rsm[:, 1:2], probs[:], axis=AX))
                fd(e, e.reciprocal(rsm[:, 2:3], rsm[:, 1:2]))
                fd(e, e.tensor_scalar(probs[:], probs[:], rsm[:, 2:3], None,
                                      ALU.mult))
                fd(e, e.max(out=tops[:], in_=probs[:]))
                cDVE.inc(e.tensor_copy(w4buf[:, 4 * tt:4 * tt + 4], tops[:, 0:4]))
                cDVE.mark(("top", tt))
            b.vector(r_v2)

            for r in range(4):
                def r_vr(e, tt=tt, r=r):
                    cDVE.wait(e, ("top", tt))
                    cDVE.inc(e.tensor_scalar(oh[:], probs[:], tops[:, r:r + 1], None,
                                             ALU.is_equal))
                    cDVE.mark(("oh", tt, r))
                b.vector(r_vr)

                def r_pr(e, tt=tt, r=r):
                    cDVE.wait(e, ("oh", tt, r))
                    if (tt, r) != (0, 0):
                        cDVE.wait(e, ("slot", tt, r - 1) if r > 0
                                  else ("slot", tt - 1, 3))
                    e.matmul(psRr[:, 0:32], lstr[:], oh[:], start=True, stop=False)
                    e.matmul(psRr[:, 0:32], onesrow[0:1, :], badd[:],
                             start=False, stop=True)
                    cPE.inc(e.matmul(psCr[0:1, 0:32], ones_f[:], oh[:],
                                     start=True, stop=True))
                    cPE.mark(("rs", tt, r))
                b.tensor(r_pr)

                def r_vs(e, tt=tt, r=r):
                    cPE.wait(e, ("rs", tt, r))
                    fd(e, e.tensor_tensor(selb[:], oh[:], psRr[:, 0:32], ALU.mult))
                    fd(e, e.reduce_sum(rsm[:, 3:4], selb[:], axis=AX))
                    fd(e, e.tensor_scalar_min(rsm[:, 3:4], rsm[:, 3:4], 32767.0))
                    fd(e, e.tensor_copy(slotbuf[:, 4 * tt + r:4 * tt + r + 1],
                                        rsm[:, 3:4]))
                    cDVE.inc(e.tensor_tensor(badd[:], badd[:], psCr[0:1, 0:32],
                                             ALU.add))
                    cDVE.mark(("slot", tt, r))
                b.vector(r_vs)

                def r_g(e, tt=tt, r=r):
                    cDVE.wait(e, ("slot", tt, r))
                    cLD.wait(e, ("xnr", tt))
                    cGL.wait(e, "xezero")
                    cGL.inc(e.indirect_dma_start(
                        d["xe_hbm"].ap()[:],
                        IndirectOffsetOnAxis(slotbuf[:, 4 * tt + r:4 * tt + r + 1], 0),
                        xnr[tt % 2][:], None), 16)
                    if r == 3:
                        cGL.mark(("scat", tt))
                b.gpsimd(r_g)
    st1.close()

    if phases < 3:
        _full_drain()
        ss.close(); sg.close(); return nc
    st2 = ExitStack()
    sh = st2.enter_context(nc.sbuf_tensor("sh", [128, NSH], BF16))
    KT = st2.enter_context(nc.sbuf_tensor("KT", [128, S], BF16))
    QT = st2.enter_context(nc.sbuf_tensor("QT", [128, S], BF16))
    V2 = st2.enter_context(nc.sbuf_tensor("V2", [128, 32 * 130], BF16))
    Vt = st2.enter_context(nc.sbuf_tensor("Vt", [128, S], BF16))
    Gbuf = st2.enter_context(nc.sbuf_tensor("Gbuf", [128, 2048], F32))
    wp = [st2.enter_context(nc.sbuf_tensor("wp0", [128, 3 * E], BF16))]
    exr = [st2.enter_context(nc.sbuf_tensor(f"exr{i}", [128, 1024], BF16))
           for i in range(3)]
    att32 = st2.enter_context(nc.sbuf_tensor("att32", [128, 1024], F32))
    rcp = st2.enter_context(nc.sbuf_tensor("rcp", [1, 2048], F32))
    svb = st2.enter_context(nc.sbuf_tensor("svb", [16, 128], F32))
    urow = st2.enter_context(nc.sbuf_tensor("urow", [1, 128], F32))
    usb = st2.enter_context(nc.sbuf_tensor("usb", [128, 1], F32))
    psSr = [st2.enter_context(nc.psum_tensor(f"psSr{i}", [128, 1024], F32))
            for i in range(3)]
    psS = psSr[0]
    psS2 = psSr[1]

    psAV = st2.enter_context(nc.psum_tensor("psAV", [128, 512], F32))
    psAVB = st2.enter_context(nc.psum_tensor("psAVB", [128, 512], F32))

    def gather_plan(pr, qt):
        out = []
        for hi in range(2):
            hh = 2 * pr + hi
            for m in range(16):
                cc = 4095 - 256 * hh + 256 * m - 32 * qt
                a_lo = max(0, cc - 2045)
                a_hi = min(31, cc)
                if a_lo > a_hi:
                    continue
                out.append((64 * hi, m, a_lo, a_hi - a_lo + 1, cc - a_lo))
        return out

    with nc.Block() as b:
        def a_init_v(e):
            cDVE.inc(e.memset(bass.AP(V2.ap().tensor, 64,
                                      [[4160, 128], [65, 64], [1, 1]]), 1.0))
            cDVE.mark("v2ones")
        b.vector(a_init_v)

        def a_init(e):
            cST.wait(e, "dflat")
            cLD.inc(e.dma_start(out=sh[:],
                                in_=dap(d["dflat"], 0, [[1, 128], [1, NSH]])), 16)
            cLD.mark("sh")
        b.sync(a_init)


        for pr in range(npr):
            bufp = 0

            def a_w(e, pr=pr):
                if pr >= 1:
                    cPE.wait(e, ("v", pr - 1, NT - 1))
                for wnm, col in (("wq", 0), ("wk", 1), ("wv", 2)):
                    cW.inc(e.dma_start(
                        out=wp[0][:, E * col:E * col + E],
                        in_=dap(d[wnm], 128 * pr,
                                [[E, 128], [128 * E, 8], [1, 128]])), 16)
                cW.mark(("wp", pr))
            b.gpsimd(a_w)

            base_dve = cDVE.n

            def a_qkv_pe(e, pr=pr, bufp=bufp, base_dve=base_dve):
                cW.wait(e, ("wp", pr))
                cLD.wait(e, "sh")
                if pr >= 1:
                    cDVE.wait_val(e, cDVE.marks[("avev", pr - 1, nqt - 1)])
                k = 0
                for proj in (0, 1):
                    for t8 in range(8):
                        if k >= 2:
                            e.wait_ge(cDVE.h, base_dve + k - 1)
                        ps = psSr[t8 % 2]
                        for et in range(8):
                            ins = e.matmul(
                                ps[:, 0:512],
                                wp[bufp][:, E * proj + 128 * et:E * proj + 128 * et + 128],
                                xnT[et][:, 512 * t8:512 * t8 + 512],
                                start=(et == 0), stop=(et == 7))
                        cPE.inc(ins)
                        cPE.mark(("qkv", pr, proj, t8))
                        k += 1
                for t32 in range(NT):
                    ps = psSr[t32 % 2]
                    if k >= 2:
                        e.wait_ge(cDVE.h, base_dve + k - 1)
                    for et in range(8):
                        ins = e.matmul(
                            ps[:, 512:640],
                            xnT[et][:, 128 * t32:128 * t32 + 128],
                            wp[bufp][:, 2 * E + 128 * et:2 * E + 128 * et + 128],
                            start=(et == 0), stop=(et == 7))
                    cPE.inc(ins)
                    cPE.mark(("v", pr, t32))
                    k += 1
            b.tensor(a_qkv_pe)

            def a_qkv_dve(e, pr=pr):
                for proj, dst in ((0, QT), (1, KT)):
                    for t8 in range(8):
                        ps = psSr[t8 % 2]
                        cPE.wait(e, ("qkv", pr, proj, t8))
                        cDVE.inc(e.tensor_copy(dst[:, 512 * t8:512 * t8 + 512],
                                               ps[:, 0:512]))
                for t32 in range(NT):
                    ps = psSr[t32 % 2]
                    cPE.wait(e, ("v", pr, t32))
                    ov = bass.AP(V2.ap().tensor, 130 * t32,
                                 [[4160, 128], [65, 2], [1, 64]])
                    iv = bass.AP(ps.ap().tensor, 512,
                                 [[1024, 128], [64, 2], [1, 64]])
                    e.tensor_copy(ov, iv)
                    cDVE.inc(e.tensor_copy(Vt[:, 128 * t32:128 * t32 + 128],
                                           ps[:, 512:640]))
                cDVE.mark(("vdone", pr))
            b.vector(a_qkv_dve)


            base_dve2 = cDVE.n

            def a_conv_pe(e, pr=pr, base_dve2=base_dve2):
                cDVE.wait(e, ("vdone", pr))
                for tc in range(4):
                    ps = psSr[tc % 2]
                    if tc >= 2:
                        e.wait_ge(cDVE.h, base_dve2 + tc - 1)
                    for r in range(32):
                        ins = e.matmul(
                            ps[:, 0:512], Vt[:, 128 * r:128 * r + 128],
                            dap(sh, 8192 * tc + 128 * r, [[NSH, 128], [16, 512]]),
                            start=(r == 0), stop=(r == 31))
                    cPE.inc(ins)
                    cPE.mark(("cv", pr, tc))
                for r in range(32):
                    ins = e.matmul(psAV[0:16, 0:128], m16[:],
                                   Vt[:, 128 * r:128 * r + 128],
                                   start=(r == 0), stop=(r == 31))
                cPE.inc(ins)
                cPE.mark(("sv", pr))
            b.tensor(a_conv_pe)

            def a_conv_dve(e, pr=pr):
                for tc in range(4):
                    ps = psSr[tc % 2]
                    cPE.wait(e, ("cv", pr, tc))
                    cDVE.inc(e.tensor_copy(Gbuf[:, 512 * tc:512 * tc + 512],
                                           ps[:, 0:512]))
                cPE.wait(e, ("sv", pr))
                cDVE.inc(e.tensor_copy(svb[:], psAV[0:16, 0:128]))
                cDVE.mark(("sv", pr))
            b.vector(a_conv_dve)

            def a_u_pe(e, pr=pr):
                cDVE.wait(e, ("sv", pr))
                cPE.inc(e.matmul(psAVB[0:1, 0:128], rtcol16[:], svb[:],
                                 start=True, stop=True))
                cPE.mark(("umm", pr))
            b.tensor(a_u_pe)

            def a_u_dve(e, pr=pr):
                cPE.wait(e, ("umm", pr))
                cDVE.inc(e.tensor_copy(urow[:], psAVB[0:1, 0:128]))
                cDVE.mark(("urow", pr))
            b.vector(a_u_dve)

            def a_u_pe2(e, pr=pr):
                cDVE.wait(e, ("urow", pr))
                cPE.inc(e.transpose(psAVB[:, 128:129], urow[:], ident_f[0:1, 0:1]))
                cPE.mark(("utr", pr))
            b.tensor(a_u_pe2)

            def a_u_dve2(e, pr=pr):
                cPE.wait(e, ("utr", pr))
                cDVE.inc(e.tensor_copy(usb[:], psAVB[:, 128:129]))
                cDVE.mark(("u", pr))
            b.vector(a_u_dve2)

            for qt in range(nqt):
                qof = 512 * qt
                base_exp = cEXP.n
                base_sc0 = cSC.n
                base_av0 = cAV.n

                def a_sc_pe(e, pr=pr, qt=qt, base_exp=base_exp, qof=qof):
                    if qt == 0:
                        cDVE.wait(e, ("u", pr))
                    if (pr, qt) != (0, 0):
                        prev = (("avev", pr, qt - 1) if qt > 0
                                else ("avev", pr - 1, nqt - 1))
                        cDVE.wait_val(e, cDVE.marks[prev])
                    for kc in range(34):
                        if kc < 32:
                            ps = psSr[kc % 3]
                            if kc >= 3:
                                e.wait_ge(cEXP.h, base_exp + kc - 2)
                            e.matmul(ps[:, 0:512],
                                     KT[0:64, 128 * kc:128 * kc + 128],
                                     QT[0:64, qof:qof + 512],
                                     start=True, stop=True, tile_position=(0, 0))
                            ins = e.matmul(ps[:, 512:1024],
                                           KT[64:128, 128 * kc:128 * kc + 128],
                                           QT[64:128, qof:qof + 512],
                                           start=True, stop=True,
                                           tile_position=(64, 0))
                            cSC.inc(ins)
                        if kc >= 2:
                            k = kc - 2
                            e.wait_ge(cEXP.h, base_exp + k + 1)
                            e.matmul(psAV[0:65, 0:512],
                                     V2[:, 130 * k:130 * k + 65],
                                     exr[k % 3][:, 0:512],
                                     start=(k == 0), stop=(k == 31),
                                     tile_position=(0, 0))
                            ins = e.matmul(psAVB[0:65, 0:512],
                                           V2[:, 130 * k + 65:130 * k + 130],
                                           exr[k % 3][:, 512:1024],
                                           start=(k == 0), stop=(k == 31),
                                           tile_position=(0, 0))
                            cAV.inc(ins)
                    cAV.mark(("av", pr, qt))
                b.tensor(a_sc_pe)

                def a_exp(e, pr=pr, qt=qt, base_sc0=base_sc0, base_av0=base_av0):
                    for k in range(32):
                        e.wait_ge(cSC.h, base_sc0 + k + 1)
                        if k >= 3:
                            e.wait_ge(cAV.h, base_av0 + k - 2)
                        cEXP.inc(e.activation(out=exr[k % 3][:], in_=psSr[k % 3][:],
                                              func=AF.Exp, scale=1.0 / 32.0))
                b.scalar(a_exp)

                def a_post_dve(e, pr=pr, qt=qt):
                    cAV.wait(e, ("av", pr, qt))
                    e.reciprocal(rcp[0:1, 0:512], psAV[64:65, 0:512])
                    cDVE.inc(e.reciprocal(rcp[0:1, 512:1024], psAVB[64:65, 0:512]))
                    cDVE.mark(("rcp", pr, qt))
                b.vector(a_post_dve)

                def a_bc_pe(e, pr=pr, qt=qt):
                    cDVE.wait(e, ("rcp", pr, qt))
                    e.matmul(psSr[0][0:64, 0:512], onesrow[0:1, 0:64],
                             rcp[0:1, 0:512],
                             start=True, stop=True, tile_position=(0, 0))
                    cPE.inc(e.matmul(psSr[0][64:128, 0:512], onesrow[0:1, 0:64],
                                     rcp[0:1, 512:1024],
                                     start=True, stop=True, tile_position=(0, 64)))
                    cPE.mark(("bc", pr, qt))
                b.tensor(a_bc_pe)

                def a_fin_dve(e, pr=pr, qt=qt):
                    cPE.wait(e, ("bc", pr, qt))
                    if (pr, qt) != (0, 0):
                        prev = (("attst", pr, qt - 1) if qt > 0
                                else ("attst", pr - 1, nqt - 1))
                        cGL.wait_val(e, cGL.marks[prev])
                    e.tensor_copy(att32[0:64, 0:512], psAV[0:64, 0:512])
                    e.tensor_copy(att32[64:128, 0:512], psAVB[0:64, 0:512])
                    cDVE.inc(e.tensor_tensor(att32[:, 0:512], att32[:, 0:512],
                                             psSr[0][:, 0:512], ALU.mult))
                    cDVE.mark(("avev", pr, qt))
                    ins_g = None
                    for (hb, m, a_lo, a_n, in_off) in gather_plan(pr, qt):
                        ov = bass.AP(att32.ap().tensor, hb * 1024 + m + 16 * a_lo,
                                     [[1024, 64], [16, a_n]])
                        gv = bass.AP(Gbuf.ap().tensor, hb * 2048 + in_off,
                                     [[2048, 64], [-1, a_n]])
                        ins_g = e.tensor_tensor(ov, ov, gv, ALU.add)
                    if ins_g is not None:
                        fd(e, ins_g)
                    cDVE.inc(e.tensor_scalar(att32[:, 0:512], att32[:, 0:512],
                                             usb[:], None, ALU.add))
                    cDVE.mark(("attc", pr, qt))
                b.vector(a_fin_dve)

                def a_st(e, pr=pr, qt=qt, qof=qof):
                    cDVE.wait(e, ("attc", pr, qt))
                    cGL.inc(e.dma_start(
                        out=dap(d["att_hbm"], pr * 128 * S + qof,
                                [[S, 128], [1, 512]]),
                        in_=att32[:, 0:512]), 16)
                    cGL.mark(("attst", pr, qt))
                b.gpsimd(a_st)

    st2.close()

    if phases < 4:
        _full_drain()
        ss.close(); sg.close(); return nc
    st3 = ExitStack()
    atl = [st3.enter_context(nc.sbuf_tensor(f"atl{i}", [128, 1024], BF16))
           for i in range(8)]
    sq = [st3.enter_context(nc.sbuf_tensor(f"sq{i}", [128, 1024], BF16))
          for i in range(2)]
    lnst = st3.enter_context(nc.sbuf_tensor("lnst", [1, 3072], F32))
    t1b = st3.enter_context(nc.sbuf_tensor("t1b", [128, 1024], F32))
    stl = st3.enter_context(nc.sbuf_tensor("stl", [128, 1024], BF16))
    sev = [st3.enter_context(nc.sbuf_tensor(f"sev{i}", [128, 1024], BF16))
           for i in range(2)]
    psS3 = st3.enter_context(nc.psum_tensor("psS3", [128, 1024], F32))
    psB1 = st3.enter_context(nc.psum_tensor("psB1", [128, 1024], F32))
    psB2 = st3.enter_context(nc.psum_tensor("psB2", [128, 1024], F32))
    psT3 = st3.enter_context(nc.psum_tensor("psT3", [128, 1024], BF16))

    with nc.Block() as b:
        for qt in range(4):
            qof = 1024 * qt

            def e_ld(e, qt=qt, qof=qof):
                if qt >= 1:
                    cDVE.wait(e, ("sdone", qt - 1))
                for pr2 in range(8):
                    cLD.inc(e.dma_start(
                        out=atl[pr2][:],
                        in_=dap(d["att_hbm"], pr2 * 128 * S + qof,
                                [[S, 128], [1, 1024]])), 16)
                    cLD.mark(("atl", qt, pr2))
            b.sync(e_ld)

            base_pe3 = cPE.n

            def e_sq(e, qt=qt, base_pe3=base_pe3):
                for pr2 in range(8):
                    cLD.wait(e, ("atl", qt, pr2))
                    if pr2 >= 2:
                        e.wait_ge(cPE.h, base_pe3 + pr2 - 1)
                    if qt >= 1:
                        cST.wait(e, ("sst", qt - 1, pr2))
                    cACT.inc(e.activation(out=sq[pr2 % 2][:], in_=atl[pr2][:],
                                          func=AF.Square))
                    cACT.mark(("sq", qt, pr2))
            b.scalar(e_sq)

            def e_stat_pe(e, qt=qt):
                if qt >= 1:
                    cDVE.wait(e, ("var3", qt - 1))
                for pr2 in range(8):
                    cACT.wait(e, ("sq", qt, pr2))
                    for h2 in range(2):
                        e.matmul(psS3[0:1, 512 * h2:512 * h2 + 512], ones_bf[:],
                                 atl[pr2][:, 512 * h2:512 * h2 + 512],
                                 start=(pr2 == 0), stop=(pr2 == 7),
                                 tile_position=(0, 0))
                        ins = e.matmul(psS3[32:33, 512 * h2:512 * h2 + 512],
                                       ones_bf[:],
                                       sq[pr2 % 2][:, 512 * h2:512 * h2 + 512],
                                       start=(pr2 == 0), stop=(pr2 == 7),
                                       tile_position=(0, 32))
                    cPE.inc(ins)
                    cPE.mark(("st3", qt, pr2))
            b.tensor(e_stat_pe)

            def e_stat_dve(e, qt=qt):
                cPE.wait(e, ("st3", qt, 7))
                e.tensor_scalar(lnst[0:1, 0:1024], psS3[0:1, 0:1024], 1.0 / E, None,
                                ALU.mult)
                e.tensor_scalar(lnst[0:1, 1024:2048], psS3[32:33, 0:1024], 1.0 / E,
                                None, ALU.mult)
                e.tensor_tensor(lnst[0:1, 2048:3072], lnst[0:1, 0:1024],
                                lnst[0:1, 0:1024], ALU.mult)
                e.tensor_sub(lnst[0:1, 1024:2048], lnst[0:1, 1024:2048],
                             lnst[0:1, 2048:3072])
                cDVE.inc(e.tensor_scalar(lnst[0:1, 1024:2048], lnst[0:1, 1024:2048],
                                         1e-3, None, ALU.add))
                cDVE.mark(("var3", qt))
            b.vector(e_stat_dve)

            def e_rstd(e, qt=qt):
                cDVE.wait(e, ("var3", qt))
                e.activation(out=lnst[0:1, 2048:3072], in_=lnst[0:1, 1024:2048],
                             func=AF.Ln)
                cACT.inc(e.activation(out=lnst[0:1, 1024:2048],
                                      in_=lnst[0:1, 2048:3072], func=AF.Exp,
                                      scale=-0.5))
                cACT.mark(("rstd3", qt))
            b.scalar(e_rstd)

            def e_ab(e, qt=qt):
                cACT.wait(e, ("rstd3", qt))
                e.tensor_tensor(lnst[0:1, 2048:3072], lnst[0:1, 0:1024],
                                lnst[0:1, 1024:2048], ALU.mult)
                cDVE.inc(e.tensor_scalar(lnst[0:1, 2048:3072],
                                         lnst[0:1, 2048:3072], -1.0, None, ALU.mult))
                cDVE.mark(("ab", qt))
            b.vector(e_ab)

            def e_bc(e, qt=qt):
                cDVE.wait(e, ("ab", qt))
                if qt >= 1:
                    cDVE.wait(e, ("s3", qt - 1, 7))
                for h2 in range(2):
                    e.matmul(psB1[:, 512 * h2:512 * h2 + 512], onesrow[0:1, :],
                             lnst[0:1, 1024 + 512 * h2:1024 + 512 * h2 + 512],
                             start=True, stop=True)
                    ins = e.matmul(psB2[:, 512 * h2:512 * h2 + 512], onesrow[0:1, :],
                                   lnst[0:1, 2048 + 512 * h2:2048 + 512 * h2 + 512],
                                   start=True, stop=True)
                cPE.inc(ins)
                cPE.mark(("bc3", qt))
            b.tensor(e_bc)

            for pr2 in range(8):
                def e_s_dve(e, qt=qt, pr2=pr2):
                    cPE.wait(e, ("bc3", qt))
                    if pr2 >= 1:
                        cPE.wait(e, ("str3", qt, pr2 - 1))
                    e.tensor_tensor(t1b[:], atl[pr2][:], psB1[:], ALU.mult)
                    e.tensor_tensor(t1b[:], t1b[:], psB2[:], ALU.add)
                    cDVE.inc(e.tensor_scalar(stl[:], t1b[:], gammaT[:, pr2:pr2 + 1],
                                             betaT[:, pr2:pr2 + 1], ALU.mult,
                                             op1=ALU.add))
                    cDVE.mark(("s3", qt, pr2))
                b.vector(e_s_dve)

                def e_s_pe(e, qt=qt, pr2=pr2):
                    cDVE.wait(e, ("s3", qt, pr2))
                    if pr2 >= 1:
                        cDVE.wait(e, ("sev", qt, pr2 - 1))
                    for c in range(8):
                        ins = e.transpose(psT3[:, 128 * c:128 * c + 128],
                                          stl[:, 128 * c:128 * c + 128], ident_bf[:])
                    cPE.inc(ins)
                    cPE.mark(("str3", qt, pr2))
                b.tensor(e_s_pe)

                def e_s_ev(e, qt=qt, pr2=pr2):
                    cPE.wait(e, ("str3", qt, pr2))
                    if pr2 >= 2:
                        cST.wait(e, ("sst", qt, pr2 - 2))
                    for c in range(8):
                        ins = e.tensor_copy(sev[pr2 % 2][:, 128 * c:128 * c + 128],
                                            psT3[:, 128 * c:128 * c + 128])
                    cDVE.inc(ins)
                    cDVE.mark(("sev", qt, pr2))
                    if pr2 == 7:
                        cDVE.mark(("sdone", qt))
                b.vector(e_s_ev)

                def e_s_st(e, qt=qt, pr2=pr2, qof=qof):
                    cDVE.wait(e, ("sev", qt, pr2))
                    for c in range(8):
                        cST.inc(e.dma_start(
                            out=dap(d["s_hbm"], (qof + 128 * c) * E + 128 * pr2,
                                    [[E, 128], [1, 128]]),
                            in_=sev[pr2 % 2][:, 128 * c:128 * c + 128]), 16)
                    cST.mark(("sst", qt, pr2))
                b.sync(e_s_st)
    st3.close()

    if phases < 5:
        _full_drain()
        ss.close(); sg.close(); return nc
    st4 = ExitStack()
    xeT = [[st4.enter_context(nc.sbuf_tensor(f"xeT{bq}_{i}", [128, CAP], BF16))
            for i in range(8)] for bq in range(2)]
    w1s = [st4.enter_context(nc.sbuf_tensor(f"w1s{bq}", [128, 4096], BF16))
           for bq in range(2)]
    w3s = [st4.enter_context(nc.sbuf_tensor(f"w3s{bq}", [128, 4096], BF16))
           for bq in range(2)]
    w2s = [st4.enter_context(nc.sbuf_tensor(f"w2s{bq}", [128, 4096], BF16))
           for bq in range(2)]
    h1s = st4.enter_context(nc.sbuf_tensor("h1s", [128, 512], F32))
    hT = [st4.enter_context(nc.sbuf_tensor(f"hT{i}", [128, CAP], BF16))
          for i in range(4)]
    oesb = st4.enter_context(nc.sbuf_tensor("oesb", [128, 1024], BF16))
    psH = [st4.enter_context(nc.psum_tensor(f"psH{i}", [128, 512], F32))
           for i in range(4)]
    psO = [st4.enter_context(nc.psum_tensor(f"psO{i}", [128, 512], F32))
           for i in range(4)]

    with nc.Block() as b:
        for ee in range(NEXP):
            bq = ee % 2

            def m_w(e, ee=ee, bq=bq):
                if ee >= 2:
                    cPE.wait(e, ("mm13", ee - 2))
                    cPE.wait(e, ("mm2", ee - 2))
                cW.inc(e.dma_start(
                    out=w1s[bq][:],
                    in_=dap(d["w1"], ee * E * HEXP,
                            [[HEXP, 128], [128 * HEXP, 8], [1, HEXP]])), 16)
                cW.inc(e.dma_start(
                    out=w3s[bq][:],
                    in_=dap(d["w3"], ee * E * HEXP,
                            [[HEXP, 128], [128 * HEXP, 8], [1, HEXP]])), 16)
                cW.inc(e.dma_start(
                    out=w2s[bq][:],
                    in_=dap(d["w2"], ee * HEXP * E,
                            [[E, 128], [128 * E, 4], [1, E]])), 16)
                cW.mark(("mw", ee))
            b.gpsimd(m_w)

            def m_xe(e, ee=ee, bq=bq):
                if ee >= 2:
                    cPE.wait(e, ("mm13", ee - 2))
                for i in range(8):
                    cXE.inc(e.dma_start(
                        out=xeT[bq][i][:],
                        in_=d["xe_hbm"].ap()[CAP * ee:CAP * (ee + 1),
                                             128 * i:128 * (i + 1)],
                        transpose=True), 16)
                cXE.mark(("xe", ee))
            b.scalar(m_xe)

            base_act4 = cACT.n
            base_dve4 = cDVE.n

            def m_pe1(e, ee=ee, bq=bq, base_act4=base_act4, base_dve4=base_dve4):
                cW.wait(e, ("mw", ee))
                cXE.wait(e, ("xe", ee))
                k = 0
                for ht in range(4):
                    for stq in range(2):
                        if k >= 2:
                            e.wait_ge(cACT.h, base_act4 + k - 1)
                        for et in range(8):
                            ins = e.matmul(
                                psH[(2 * k) % 4][:],
                                w1s[bq][:, 512 * et + 128 * ht:512 * et + 128 * ht + 128],
                                xeT[bq][et][:, 512 * stq:512 * stq + 512],
                                start=(et == 0), stop=(et == 7))
                        cPE.inc(ins)
                        cPE.mark(("h1", ee, ht, stq))
                        if k >= 2:
                            e.wait_ge(cDVE.h, base_dve4 + k - 1)
                        for et in range(8):
                            ins = e.matmul(
                                psH[(2 * k + 1) % 4][:],
                                w3s[bq][:, 512 * et + 128 * ht:512 * et + 128 * ht + 128],
                                xeT[bq][et][:, 512 * stq:512 * stq + 512],
                                start=(et == 0), stop=(et == 7))
                        cPE.inc(ins)
                        cPE.mark(("h3", ee, ht, stq))
                        k += 1
                cPE.mark(("mm13", ee))
            b.tensor(m_pe1)

            def m_silu(e, ee=ee, base_dve4=base_dve4):
                for k in range(8):
                    ht, stq = divmod(k, 2)
                    cPE.wait(e, ("h1", ee, ht, stq))
                    if k >= 1:
                        e.wait_ge(cDVE.h, base_dve4 + k)
                    cACT.inc(e.activation(out=h1s[:], in_=psH[(2 * k) % 4][:],
                                          func=AF.Silu))
                    cACT.mark(("sl", ee, k))
            b.scalar(m_silu)

            def m_hT(e, ee=ee):
                for k in range(8):
                    ht, stq = divmod(k, 2)
                    cACT.wait(e, ("sl", ee, k))
                    cPE.wait(e, ("h3", ee, ht, stq))
                    cDVE.inc(e.tensor_tensor(hT[ht][:, 512 * stq:512 * stq + 512],
                                             h1s[:], psH[(2 * k + 1) % 4][:],
                                             ALU.mult))
                cDVE.mark(("hT", ee))
            b.vector(m_hT)

            base_dve5 = cDVE.n

            def m_pe2(e, ee=ee, bq=bq, base_dve5=base_dve5):
                cDVE.wait(e, ("hT", ee))
                for st8 in range(8):
                    for dt in range(2):
                        j = 2 * st8 + dt
                        if j >= 4:
                            e.wait_ge(cDVE.h, base_dve5 + j - 3)
                        for ht in range(4):
                            ins = e.matmul(
                                psO[j % 4][:],
                                hT[ht][:, 128 * st8:128 * st8 + 128],
                                w2s[bq][:, 1024 * ht + 512 * dt:1024 * ht + 512 * dt + 512],
                                start=(ht == 0), stop=(ht == 3))
                        cPE.inc(ins)
                        cPE.mark(("o", ee, st8, dt))
                cPE.mark(("mm2", ee))
            b.tensor(m_pe2)

            base_st4 = cST.n

            def m_oev(e, ee=ee, base_st4=base_st4):
                for st8 in range(8):
                    if st8 >= 1:
                        e.wait_ge(cST.h, base_st4 + 16 * st8)
                    for dt in range(2):
                        cPE.wait(e, ("o", ee, st8, dt))
                        cDVE.inc(e.tensor_copy(oesb[:, 512 * dt:512 * dt + 512],
                                               psO[(2 * st8 + dt) % 4][:]))
                    cDVE.mark(("orow", ee, st8))
            b.vector(m_oev)

            def m_ost(e, ee=ee):
                for st8 in range(8):
                    cDVE.wait(e, ("orow", ee, st8))
                    cST.inc(e.dma_start(
                        out=d["oute_hbm"].ap()[CAP * ee + 128 * st8:
                                               CAP * ee + 128 * (st8 + 1), :],
                        in_=oesb[:]), 16)
                    cST.mark(("ost", ee, st8))
            b.sync(m_ost)
    st4.close()

    if phases < 6:
        _full_drain()
        ss.close(); sg.close(); return nc
    st5 = ExitStack()
    ws1s = st5.enter_context(nc.sbuf_tensor("ws1s", [128, 8192], BF16))
    ws3s = st5.enter_context(nc.sbuf_tensor("ws3s", [128, 8192], BF16))
    ws2s = st5.enter_context(nc.sbuf_tensor("ws2s", [128, 8192], BF16))
    hs1 = st5.enter_context(nc.sbuf_tensor("hs1", [128, 512], F32))
    hsT = [st5.enter_context(nc.sbuf_tensor(f"hsT{i}", [128, 512], BF16))
           for i in range(8)]
    yg = [st5.enter_context(nc.sbuf_tensor(f"yg{i}", [128, 1024], F32))
          for i in range(4)]
    ysb = st5.enter_context(nc.sbuf_tensor("ysb", [128, 1024], F32))
    gsb = st5.enter_context(nc.sbuf_tensor("gsb", [128, 1024], BF16))
    ssb = [st5.enter_context(nc.sbuf_tensor(f"ssb{i}", [128, 1024], BF16))
           for i in range(2)]
    xsb = [st5.enter_context(nc.sbuf_tensor(f"xsb{i}", [128, 1024], F32))
           for i in range(2)]
    osb = [st5.enter_context(nc.sbuf_tensor(f"osb{i}", [128, 1024], F32))
           for i in range(2)]
    psQ = [st5.enter_context(nc.psum_tensor(f"psQ{i}", [128, 512], F32))
           for i in range(4)]
    psZ = [st5.enter_context(nc.psum_tensor(f"psZ{i}", [128, 512], F32))
           for i in range(2)]

    with nc.Block() as b:
        def f_w(e):
            for nm, t in (("ws1", ws1s), ("ws3", ws3s), ("ws2", ws2s)):
                cW.inc(e.dma_start(
                    out=t[:],
                    in_=dap(d[nm], 0, [[E, 128], [128 * E, 8], [1, E]])), 16)
            cW.mark("wsh")
        b.gpsimd(f_w)

        for tt in range(8):
            tof = 512 * tt
            base_dve6 = cDVE.n

            def f_pe1(e, tt=tt, tof=tof, base_dve6=base_dve6):
                cW.wait(e, "wsh")
                for hst in range(8):
                    if hst >= 2:
                        e.wait_ge(cDVE.h, base_dve6 + hst - 1)
                    for et in range(8):
                        e.matmul(psQ[(2 * hst) % 4][:],
                                 ws1s[:, 1024 * et + 128 * hst:1024 * et + 128 * hst + 128],
                                 xnT[et][:, tof:tof + 512],
                                 start=(et == 0), stop=(et == 7))
                    ins = None
                    for et in range(8):
                        ins = e.matmul(psQ[(2 * hst + 1) % 4][:],
                                       ws3s[:, 1024 * et + 128 * hst:1024 * et + 128 * hst + 128],
                                       xnT[et][:, tof:tof + 512],
                                       start=(et == 0), stop=(et == 7))
                    cPE.inc(ins)
                    cPE.mark(("hs", tt, hst))
            if ks5 >= 2: b.tensor(f_pe1)

            def f_silu(e, tt=tt, base_dve6=base_dve6):
                for hst in range(8):
                    cPE.wait(e, ("hs", tt, hst))
                    if hst >= 1:
                        e.wait_ge(cDVE.h, base_dve6 + hst)
                    cACT.inc(e.activation(out=hs1[:], in_=psQ[(2 * hst) % 4][:],
                                          func=AF.Silu))
                    cACT.mark(("hsl", tt, hst))
            if ks5 >= 2: b.scalar(f_silu)

            def f_hs(e, tt=tt):
                for hst in range(8):
                    cACT.wait(e, ("hsl", tt, hst))
                    cDVE.inc(e.tensor_tensor(hsT[hst][:], hs1[:],
                                             psQ[(2 * hst + 1) % 4][:], ALU.mult))
                cDVE.mark(("hsd", tt))
            if ks5 >= 2: b.vector(f_hs)

            for tsub in range(4):
                ttg = 4 * tt + tsub

                def f_gthr(e, ttg=ttg):
                    if ttg >= 1 and ks5 >= 6:
                        cDVE.wait(e, ("yw", ttg - 1))
                    for r in range(4):
                        cGL.inc(e.indirect_dma_start(
                            yg[r][:], None, d["oute_hbm"].ap()[:],
                            IndirectOffsetOnAxis(
                                slotbuf[:, 4 * ttg + r:4 * ttg + r + 1], 0)), 16)
                    cGL.mark(("yg", ttg))
                if ks5 >= 3: b.gpsimd(f_gthr)

                def f_ld(e, ttg=ttg):
                    if ttg >= 2 and ks5 >= 6:
                        cDVE.wait(e, ("sgm", ttg - 2))
                    cLD.inc(e.dma_start(
                        out=ssb[ttg % 2][:],
                        in_=d["s_hbm"].ap()[128 * ttg:128 * (ttg + 1), :]), 16)
                    cLD.inc(e.dma_start(
                        out=xsb[ttg % 2][:],
                        in_=d["x"].ap()[128 * ttg:128 * (ttg + 1), :]), 16)
                    cLD.mark(("sx", ttg))
                if ks5 >= 4: b.sync(f_ld)

                def f_pe2(e, tt=tt, tsub=tsub, ttg=ttg):
                    cDVE.wait(e, ("hsd", tt))
                    if ttg >= 1 and ks5 >= 6:
                        cDVE.wait(e, ("zc", ttg - 1))
                    for dt in range(2):
                        for hst in range(8):
                            ins = e.matmul(
                                psZ[dt][:],
                                hsT[hst][:, 128 * tsub:128 * tsub + 128],
                                ws2s[:, 1024 * hst + 512 * dt:1024 * hst + 512 * dt + 512],
                                start=(hst == 0), stop=(hst == 7))
                    cPE.inc(ins)
                    cPE.mark(("z", ttg))
                if ks5 >= 5: b.tensor(f_pe2)

                def f_comb(e, ttg=ttg):
                    if ks5 >= 3:
                        cGL.wait(e, ("yg", ttg))
                    if ttg >= 1:
                        cACT.wait(e, ("g", ttg - 1))
                    e.tensor_scalar(ysb[:], yg[0][:], w4buf[:, 4 * ttg:4 * ttg + 1],
                                    None, ALU.mult)
                    for r in range(1, 4):
                        e.scalar_tensor_tensor(
                            ysb[:], yg[r][:],
                            w4buf[:, 4 * ttg + r:4 * ttg + r + 1], ysb[:],
                            ALU.mult, ALU.add)
                    cPE.wait(e, ("z", ttg))
                    e.tensor_tensor(ysb[:, 0:512], ysb[:, 0:512], psZ[0][:], ALU.add)
                    cDVE.inc(e.tensor_tensor(ysb[:, 512:1024], ysb[:, 512:1024],
                                             psZ[1][:], ALU.add))
                    cDVE.mark(("zc", ttg))
                    cDVE.mark(("yw", ttg))
                if ks5 >= 6: b.vector(f_comb)

                def f_g(e, ttg=ttg):
                    cDVE.wait(e, ("zc", ttg))
                    cACT.inc(e.activation(out=gsb[:], in_=ysb[:], func=AF.Silu))
                    cACT.mark(("g", ttg))
                if ks5 >= 6: b.scalar(f_g)

                def f_out(e, ttg=ttg):
                    cACT.wait(e, ("g", ttg))
                    if ks5 >= 4:
                        cLD.wait(e, ("sx", ttg))
                    if ttg >= 2:
                        cOUT.wait(e, ("ostc", ttg - 2))
                    e.tensor_tensor(osb[ttg % 2][:], ssb[ttg % 2][:], gsb[:], ALU.mult)
                    cDVE.inc(e.tensor_tensor(osb[ttg % 2][:], osb[ttg % 2][:],
                                             xsb[ttg % 2][:], ALU.add))
                    cDVE.mark(("o5", ttg))
                    cDVE.mark(("sgm", ttg))
                if ks5 >= 6: b.vector(f_out)

                def f_ost(e, ttg=ttg):
                    cDVE.wait(e, ("o5", ttg))
                    cOUT.inc(e.dma_start(
                        out=d["out"].ap()[128 * ttg:128 * (ttg + 1), :],
                        in_=osb[ttg % 2][:]), 16)
                    cOUT.mark(("ostc", ttg))
                if ks5 >= 6: b.sync(f_ost)

        def f_fin(e):
            e.wait_ge(cOUT.h, cOUT.n)
        b.sync(f_fin)
    st5.close()
    _full_drain()
    ss.close()
    sg.close()
    return nc


def _consts():
    c = {}
    c["ident_bf"] = np.eye(128, dtype=np.float32).astype(BF)
    c["ident_f32"] = np.eye(128, dtype=np.float32)
    c["lstr_f32"] = np.triu(np.ones((128, 128), np.float32), 1)
    c["ones_bf"] = np.ones((128, 1), np.float32).astype(BF)
    c["ones_f32"] = np.ones((128, 1), np.float32)
    c["onesrow_f32"] = np.ones((1, 128), np.float32)
    c["m16_bf"] = (np.arange(128)[:, None] % 16 ==
                   np.arange(16)[None, :]).astype(np.float32).astype(BF)
    c["e1024"] = (1024.0 * np.arange(32, dtype=np.float32)).reshape(1, 32)
    return c


def kernel(L0_attn_x, L0_attn_Wq, L0_attn_Wk, L0_attn_Wv, L0_attn_rel_table,
           L0_attn_gamma, L0_attn_beta, L0_ffn_Wg, L0_ffn_W1, L0_ffn_W2,
           L0_ffn_W3, L0_ffn_Ws1, L0_ffn_Ws2, L0_ffn_Ws3):
    nc = build_nc()
    c = _consts()
    shared = dict(
        wq=np.ascontiguousarray(L0_attn_Wq), wk=np.ascontiguousarray(L0_attn_Wk),
        wv=np.ascontiguousarray(L0_attn_Wv),
        rel=np.ascontiguousarray(L0_attn_rel_table),
        gamma_r=np.ascontiguousarray(np.asarray(L0_attn_gamma).reshape(8, 128)),
        beta_r=np.ascontiguousarray(np.asarray(L0_attn_beta).reshape(8, 128)),
        wg=np.ascontiguousarray(L0_ffn_Wg),
        w1=np.ascontiguousarray(L0_ffn_W1), w2=np.ascontiguousarray(L0_ffn_W2),
        w3=np.ascontiguousarray(L0_ffn_W3),
        ws1=np.ascontiguousarray(L0_ffn_Ws1), ws2=np.ascontiguousarray(L0_ffn_Ws2),
        ws3=np.ascontiguousarray(L0_ffn_Ws3),
        **c)
    in_maps = []
    for bi in range(8):
        m = dict(shared)
        m["x"] = np.ascontiguousarray(L0_attn_x[bi])
        in_maps.append(m)
    res = run_bass_kernel_spmd(nc, in_maps, core_ids=list(range(8)))
    return np.stack([np.asarray(r["out"]) for r in res.results], axis=0)
